# version 16 (final)
# speedup vs baseline: 1.0526x; 1.0258x over previous
import numpy as np
import ml_dtypes
from contextlib import ExitStack

import concourse.bass as bass
import concourse.mybir as mybir
from concourse.bass_utils import run_bass_kernel_spmd

F32 = mybir.dt.float32
BF16 = mybir.dt.bfloat16
AF = mybir.ActivationFunctionType
ALU = mybir.AluOpType
AX = mybir.AxisListType

D = 2048
T = 2048
NT = 16
HV, HK, DK, DV = 32, 16, 128, 128
NH, NOPE, VD = 16, 128, 128
QHD = 192
E_MOE, I_MOE = 16, 1408
SHI = 2816
LN_EPS = 1e-5
EPS = 1e-6

C_LNEPS, C_EPS, C_NHALF, C_QSCL, C_INV_D, C_INV_DV, C_NEG1E9, C_PBIAS, \
    C_FSCale, C_NEG1, C_HALF = range(11)


class Sem:
    def __init__(self, ctx, nc, name, b=None):
        self.h = ctx.enter_context(nc.semaphore(name))
        self.n = 0
        self.b = b

    def inc(self, inst, k=1):
        inst.then_inc(self.h, k)
        self.n += k
        return inst

    def dma(self, inst):
        return self.inc(inst, 16)

    def wait(self, eng, n=None):
        eng.wait_ge(self.h, self.n if n is None else n)


class B:

    def __init__(self, dbg=(), p1b_chunks=NT, p1b_pairs=8, p1b_steps=999,
                 p1b_nogram=False, p1b_notr=False):
        self.p1b_chunks = p1b_chunks
        self.p1b_pairs = p1b_pairs
        self.p1b_steps = p1b_steps
        self.p1b_nogram = p1b_nogram
        self.p1b_notr = p1b_notr
        self.dbg = set(dbg)
        self.nc = bass.Bass("TRN2", target_bir_lowering=False, debug=False)
        self.ctx = ExitStack()
        self.phase_n = 0
        self._undrained = {}
        self._inst_eng = {}

    def din(self, name, shape, dt=F32):
        return self.nc.dram_tensor(name, shape, dt, kind="ExternalInput").ap()

    def dscr(self, name, shape, dt=F32):
        kind = "ExternalOutput" if name in self.dbg else "Internal"
        return self.nc.dram_tensor(name, shape, dt, kind=kind).ap()

    def sb(self, name, shape, dt=F32, ctx=None):
        return (ctx or self.ctx).enter_context(
            self.nc.sbuf_tensor(name, shape, dt))

    def ps(self, name, shape, dt=F32, ctx=None):
        return (ctx or self.ctx).enter_context(
            self.nc.psum_tensor(name, shape, dt))

    def sem(self, name):
        return Sem(self.ctx, self.nc, name, b=self)

    _DRAIN_OPS = {
        "tensor_copy", "tensor_tensor", "tensor_add", "tensor_sub",
        "tensor_mul", "tensor_max", "tensor_scalar", "tensor_single_scalar",
        "tensor_scalar_add", "tensor_scalar_sub", "tensor_scalar_mul",
        "tensor_scalar_max", "tensor_scalar_min", "tensor_relu",
        "tensor_reduce", "reduce_max", "reduce_sum", "scalar_tensor_tensor",
        "memset", "memzero", "affine_select", "reciprocal", "activation",
        "copy", "mul", "add", "square", "sqrt", "sign", "select",
        "copy_predicated", "transpose", "bn_stats", "bn_aggr", "iota",
        "stream_shuffle", "engine_nop",
    }

    @staticmethod
    def _ap_span(a):
        if not isinstance(a, bass.AP):
            return None
        try:
            dims = a.ap
            pstride, pnum = dims[0]
            off = a.offset
            p0, f0 = (off // pstride, off % pstride) if pstride else (0, off)
            fspan = sum((n - 1) * abs(s) for s, n in dims[1:])
        except (AttributeError, TypeError, ZeroDivisionError):
            return None
        t = a.tensor
        return (getattr(t, "name", None) or id(t),
                p0, p0 + pnum - 1, f0, f0 + fspan)

    class _DrainProxy:

        def __init__(self, eng, b, always):
            self._eng = eng
            self._b = b
            self._always = always

        def __getattr__(self, name):
            real = getattr(self._eng, name)
            if name in B._DRAIN_OPS:
                if self._always:
                    def wrapped(*a, **k):
                        ins = real(*a, **k)
                        self._eng.drain()
                        return ins
                else:
                    def wrapped(*a, **k):
                        out_ap = k.get("out")
                        if out_ap is None and a:
                            out_ap = a[0]
                        wr = B._ap_span(out_ap)
                        rd = [s for s in (B._ap_span(x) for x in
                                          list(a) + list(k.values())
                                          if x is not out_ap) if s]
                        key = id(self._eng)
                        und = self._b._undrained.setdefault(key, [])

                        def ov(x, y):
                            return (x is not None and y is not None and
                                    x[0] == y[0] and
                                    x[1] <= y[2] and y[1] <= x[2] and
                                    x[3] <= y[4] and y[3] <= x[4])
                        acc = rd + ([wr] if wr else [])
                        conflict = any(
                            any(ov(pw, s) for s in acc) or
                            any(ov(wr, s) for s in pr_)
                            for pw, pr_ in und)
                        if conflict:
                            self._eng.drain()
                            und.clear()
                        ins = real(*a, **k)
                        und.append((wr, tuple(rd)))
                        self._b._inst_eng[id(ins)] = self._eng
                        return ins
                return wrapped
            return real

    def on(self, engname):
        def deco(fn):
            if engname in ("vector", "scalar"):
                getattr(self.block, engname)(
                    lambda eng: fn(B._DrainProxy(eng, self, False)))
            elif engname == "gpsimd":
                getattr(self.block, engname)(
                    lambda eng: fn(B._DrainProxy(eng, self, True)))
            else:
                getattr(self.block, engname)(fn)
        return deco

    def barrier(self):
        self.phase_n += 1
        tgt = 5 * self.phase_n
        for en in ("sync", "gpsimd", "scalar", "vector", "tensor"):
            @self.on(en)
            def _(eng, en=en):
                if en in ("scalar", "vector"):
                    eng.drain()
                eng.nop().then_inc(self.sPH.h, 1)
                eng.wait_ge(self.sPH.h, tgt)

    def build(self):
        nc = self.nc
        self.ctx.__enter__()
        E = self.ctx

        self.x_own = self.din("x_own", [T, D])
        self.x_oth = self.din("x_oth", [T, D])
        self.cst_in = self.din("cst", [128, 16])
        self.ln1_in = self.din("ln1", [1, D])
        self.ln2_in = self.din("ln2", [1, D])
        self.normw_in = self.din("normw", [128, 1])
        self.dtb16_in = self.din("dtb16", [1, 512])
        self.alog16_in = self.din("alog16", [1, 512])
        self.convw_in = self.din("convw", [8192, 4])
        self.wqkv_in = self.din("wqkv", [64, 128, NT * 128])
        self.wz_in = self.din("wz", [32, 128, NT * 128])
        self.wab_in = self.din("wab", [128, NT * 64])
        self.wout_in = self.din("wout", [4, 128, HV * 512])
        self.qw_in = self.din("qw", [NH, 128, NT * 128])
        self.kvk_in = self.din("kvk", [NH, 128, NT * 128])
        self.kvv_in = self.din("kvv", [NH, 128, NT * 128])
        self.ow_in = self.din("ow", [128, NH * T])
        self.gw_in = self.din("gw", [128, NT * 16])
        self.w1_in = self.din("w1r", [E_MOE, 22, 128, NT * 128])
        self.w2_in = self.din("w2r", [E_MOE, 128, 11 * T])
        self.shg_in = self.din("shgr", [44, 128, NT * 128])
        self.shd_in = self.din("shdr", [2, 128, 11 * T])
        self.sel_in = self.din("sel", [32, 32 * 128])
        self.id4_in = self.din("id4", [128, 512])
        self.mskui4_in = self.din("mskui4", [128, 512], BF16)
        self.mskus4_in = self.din("mskus4", [128, 512], BF16)
        self.mskls4_in = self.din("mskls4", [128, 512], BF16)
        self.tieb_in = self.din("tieb", [128, 16])
        self.cuseq_in = self.din("cuseq", [1, 8], mybir.dt.int32)
        self.pos_in = self.din("pos", [1, 64], mybir.dt.int32)
        self.out_t = self.nc.dram_tensor("out", [T, D], F32,
                                         kind="ExternalOutput").ap()

        self.XN32 = self.dscr("XN32", [T, D])
        self.XNBF = self.dscr("XNBF", [T, D], BF16)
        self.X1BF = self.dscr("X1BF", [T, D], BF16)
        self.X1BFo = self.dscr("X1BFo", [T, D], BF16)
        self.MX = self.dscr("MX", [8192, T], BF16)
        self.ZS = self.dscr("ZS", [HV * DV, T], BF16)
        self.YS = self.dscr("YS", [128, NT * HV * 128], BF16)
        self.R1 = self.dscr("R1", [T, D])
        self.KF = self.dscr("KF", [NH, NOPE, 2 * T], BF16)
        self.VTs = self.dscr("VTs", [NH, 2 * T, VD], BF16)
        self.QS = self.dscr("QS", [NH, NOPE, T], BF16)
        self.OFs = self.dscr("OFs", [NH, VD, T], BF16)
        self.X2BF = self.dscr("X2BF", [T, D], BF16)
        self.RSCR = self.dscr("RSCR", [T, D])
        self.ACC = self.dscr("ACC", [T, D])
        self.ACC2 = self.dscr("ACC2", [T, D])
        self.DBG_QKN = self.dscr("DBG_QKN", [128, 4096], BF16)
        self.DBG_AT = self.dscr("DBG_AT", [128, 128], BF16)
        self.DBG_NP = self.dscr("DBG_NP", [128, 128], BF16)
        self.DBG_NN = self.dscr("DBG_NN", [128, 128], BF16)
        self.DBG_MP = self.dscr("DBG_MP", [128, 128], BF16)
        self.DBG_UB = self.dscr("DBG_UB", [128, 128], BF16)
        self.DBG_DL = self.dscr("DBG_DL", [128, 128], BF16)
        self.DBG_E1 = self.dscr("DBG_E1", [128, 128])
        self.DBG_WF = self.dscr("DBG_WF", [128, NT * 16])
        self.DBG_ACT = self.dscr("DBG_ACT", [128, 11 * T], BF16)

        self.cst = self.sb("cstS", [128, 16])
        self.ln2b = self.sb("ln2b", [128, D])
        self.normw = self.sb("normwS", [128, 1])
        self.ident = self.sb("identS", [128, 128], BF16)
        self.ident32 = self.sb("ident32S", [128, 128], F32)

        self.onesr_f = self.sb("onesr_f", [1, 128])
        self.onesr_b = self.sb("onesr_b", [1, 128], BF16)
        self.onesc_f = self.sb("onesc_f", [128, 1])
        self.onesc_b = self.sb("onesc_b", [128, 1], BF16)
        self.e127 = self.sb("e127", [128, 1])
        self.intmeta = self.sb("intmetaS", [1, 72], mybir.dt.int32)
        self.fmask = [self.sb(f"fmask{r}", [128, 512], BF16) for r in range(4)]
        self.kmax = self.sb("kmaxS", [1, NH])
        self.px_delta = ExitStack()
        self.tri32 = self.sb("tri32", [128, 128], ctx=self.px_delta)
        self.dtb_bc = self.sb("dtb_bc", [128, 512], ctx=self.px_delta)
        self.nega_bc = self.sb("nega_bc", [128, 512], ctx=self.px_delta)
        self.G32 = self.sb("G32", [128, 512], ctx=self.px_delta)
        self.BET = self.sb("BET", [128, 512], ctx=self.px_delta)
        self.LB = self.sb("LB", [128, 512], ctx=self.px_delta)
        self.C32 = self.sb("C32", [128, 512], ctx=self.px_delta)
        self.GAM = self.sb("GAM", [128, 512], ctx=self.px_delta)
        self.CLB = self.sb("CLB", [128, 512], ctx=self.px_delta)
        self.sel = self.sb("selS", [32, 32 * 128], ctx=self.px_delta)
        self.id4 = self.sb("id4S", [128, 512], F32, ctx=self.px_delta)
        self.mskui4 = self.sb("mskui4S", [128, 512], BF16, ctx=self.px_delta)
        self.mskus4 = self.sb("mskus4S", [128, 512], BF16, ctx=self.px_delta)
        self.mskls4 = self.sb("mskls4S", [128, 512], BF16, ctx=self.px_delta)
        self.S32 = self.sb("S32", [128, HV * DV], ctx=self.px_delta)
        self.S0b = self.sb("S0bS", [128, HV * DV], BF16, ctx=self.px_delta)
        self.px_ln1 = ExitStack()
        self.ln1b = self.sb("ln1b", [128, D], ctx=self.px_ln1)

        self.sA = self.sem("sA")
        self.sLx = [self.sem("sLx0"), self.sem("sLx1")]
        self.sSc = [self.sem("sSc0"), self.sem("sSc1")]
        self.sXF = self.sem("sXF")
        self.sWab = self.sem("sWab")
        self.sW = [self.sem("sW0"), self.sem("sW1")]
        self.sMXs = [self.sem("sMXs0"), self.sem("sMXs1")]
        self.sMxL = self.sem("sMxL")
        self.sYst = self.sem("sYst")
        self.sDBG = self.sem("sDBG")
        self.sWob = [self.sem("sWob0"), self.sem("sWob1")]
        self.sYld = [self.sem("sYld0"), self.sem("sYld1")]
        self.sXld = [self.sem("sXld0"), self.sem("sXld1")]
        self.sR1 = [self.sem("sR10"), self.sem("sR11")]
        self.sT = self.sem("sT")
        self.sV = self.sem("sV")
        self.sS = self.sem("sS")
        self.sG = self.sem("sG")
        self.sPH = self.sem("sPH")

        self.block = E.enter_context(nc.Block())

        self.emit_consts()
        self.barrier()
        self.emit_p0()
        self.barrier()
        self.px_ln1.close()
        self.emit_p1a()
        self.barrier()
        self.emit_p1b()
        self.barrier()
        self.px_delta.close()
        self.emit_p1c()
        self.barrier()
        self.emit_f1()
        self.barrier()
        self.emit_f2()
        self.barrier()
        self.emit_f3()
        self.barrier()
        self.emit_p4()
        self.barrier()

        self.ctx.__exit__(None, None, None)
        self.nc.finalize()
        return self.nc

    def emit_consts(self):
        b = self
        self._cst_px = ExitStack()
        p_bc = [b.ps(f"cst_pbc{i}", [128, 512], F32, self._cst_px)
                for i in range(2)]

        @b.on("sync")
        def _(sync):
            b.sA.dma(sync.dma_start(out=b.cst[:], in_=b.cst_in))
            b.sA.dma(sync.dma_start(out=b.normw[:], in_=b.normw_in))
            b.sA.dma(sync.dma_start(out=b.ln1b[0:1, :], in_=b.ln1_in))
            b.sA.dma(sync.dma_start(out=b.ln2b[0:1, :], in_=b.ln2_in))
            b.sA.dma(sync.dma_start(out=b.dtb_bc[0:1, :], in_=b.dtb16_in))
            b.sA.dma(sync.dma_start(out=b.nega_bc[0:1, :], in_=b.alog16_in))
            b.sA.dma(sync.dma_start(out=b.sel[:], in_=b.sel_in))
            b.sA.dma(sync.dma_start(out=b.id4[:], in_=b.id4_in))
            b.sA.dma(sync.dma_start(out=b.mskui4[:], in_=b.mskui4_in))
            b.sA.dma(sync.dma_start(out=b.mskus4[:], in_=b.mskus4_in))
            b.sA.dma(sync.dma_start(out=b.mskls4[:], in_=b.mskls4_in))
            b.sA.dma(sync.dma_start(out=b.intmeta[0:1, 0:8], in_=b.cuseq_in))
            b.sA.dma(sync.dma_start(out=b.intmeta[0:1, 8:72], in_=b.pos_in))

        @b.on("gpsimd")
        def _(gp):
            def paired(buf, init, fill_val, compare, base, pattern, cm):
                b.sG.inc(gp.memset(buf, init))
                gp.wait_ge(b.sG.h, b.sG.n)
                b.sG.inc(gp.affine_select(out=buf, in_=buf, compare_op=compare,
                                          fill=fill_val, base=base,
                                          pattern=pattern,
                                          channel_multiplier=cm))
                gp.wait_ge(b.sG.h, b.sG.n)

            paired(b.ident[:], 0.0, 1.0, ALU.not_equal, 0, [[-1, 128]], 1)
            paired(b.ident32[:], 0.0, 1.0, ALU.not_equal, 0, [[-1, 128]], 1)
            paired(b.tri32[:], 1.0, 0.0, ALU.is_ge, 0, [[1, 128]], -1)
            gp.memset(b.onesr_f[:], 1.0)
            gp.memset(b.onesr_b[:], 1.0)
            gp.memset(b.onesc_f[:], 1.0)
            gp.memset(b.onesc_b[:], 1.0)
            gp.memset(b.S32[:], 0.0)
            gp.memset(b.S0b[:], 0.0)
            paired(b.e127[:], 0.0, 1.0, ALU.not_equal, -127, [[0, 1]], 1)
            for r in range(4):
                paired(b.fmask[r][:], 1.0, 0.0, ALU.is_ge, -(r * 128),
                       [[1, 512]], -1)
            b.sG.inc(gp.engine_nop())
            gp.wait_ge(b.sG.h, b.sG.n)

        jobs = []
        for dst, n512 in ((b.ln1b, 4), (b.ln2b, 4),
                          (b.dtb_bc, 1), (b.nega_bc, 1)):
            for c in range(n512):
                jobs.append((dst, c))
        vmark = {}
        for j, (dst, c) in enumerate(jobs):
            p = p_bc[j % 2]

            @b.on("tensor")
            def _(tensor, dst=dst, c=c, p=p, j=j):
                if j < 2:
                    b.sA.wait(tensor)
                    b.sG.wait(tensor)
                if j >= 2:
                    tensor.wait_ge(b.sV.h, vmark[j - 2])
                b.sT.inc(tensor.matmul(p[:], b.onesr_f[:],
                                       dst[0:1, c * 512:(c + 1) * 512],
                                       start=True, stop=True))
            tmark = b.sT.n

            @b.on("vector")
            def _(vector, dst=dst, c=c, p=p, tmark=tmark):
                vector.wait_ge(b.sT.h, tmark)
                b.sV.inc(vector.tensor_copy(
                    dst[:, c * 512:(c + 1) * 512], p[:]))
            vmark[j] = b.sV.n

    def emit_p0(self):
        b = self
        self._cst_px.close()
        with ExitStack() as px:
            xt = [b.sb(f"p0_xt{i}", [128, D], F32, px) for i in range(2)]
            xnt = [b.sb(f"p0_xnt{i}", [128, D], F32, px) for i in range(2)]
            sq = [b.sb(f"p0_sq{i}", [128, D], F32, px) for i in range(2)]
            xnb = [b.sb(f"p0_xnb{i}", [128, D], BF16, px) for i in range(2)]
            x1b = [b.sb(f"p0_x1b{i}", [128, D], BF16, px) for i in range(2)]
            stat = b.sb("p0_stat", [128, 128], F32, px)

            mk = {}

            def emit_load(i):
                own = i < NT
                tt = i % NT
                p = i % 2
                src = b.x_own if own else b.x_oth
                row = slice(tt * 128, (tt + 1) * 128)

                @b.on("sync")
                def _(sync, i=i, p=p, src=src, row=row):
                    if i >= 2:
                        sync.wait_ge(b.sV.h, mk[("xfree", i - 2)])
                    b.sLx[p].dma(sync.dma_start(out=xt[p][:], in_=src[row, :]))
                mk[("xld", i)] = (b.sLx[p], b.sLx[p].n)

            emit_load(0)
            for i in range(2 * NT):
                own = i < NT
                tt = i % NT
                p = i % 2
                st = 4 * i
                row = slice(tt * 128, (tt + 1) * 128)
                if i + 1 < 2 * NT:
                    emit_load(i + 1)

                @b.on("scalar")
                def _(scalar, i=i, p=p):
                    scalar.wait_ge(mk[("xld", i)][0].h, mk[("xld", i)][1])
                    if i >= 2:
                        scalar.wait_ge(b.sV.h, mk[("sqfree", i - 2)])
                    b.sS.inc(scalar.activation(out=sq[p][:], in_=xt[p][:],
                                               func=AF.Square))
                mk[("sq", i)] = b.sS.n

                @b.on("vector")
                def _(vector, i=i, p=p, st=st):
                    vector.wait_ge(mk[("xld", i)][0].h, mk[("xld", i)][1])
                    vector.reduce_sum(stat[:, st:st + 1], xt[p][:], axis=AX.X)
                    vector.wait_ge(b.sS.h, mk[("sq", i)])
                    vector.reduce_sum(stat[:, st + 1:st + 2], sq[p][:],
                                      axis=AX.X)
                    vector.tensor_single_scalar(
                        stat[:, st + 3:st + 4], stat[:, st:st + 1],
                        b.cst[:, C_INV_D:C_INV_D + 1], op=ALU.mult)
                    vector.tensor_mul(stat[:, st:st + 1],
                                      stat[:, st + 3:st + 4],
                                      stat[:, st + 3:st + 4])
                    b.sV.inc(vector.scalar_tensor_tensor(
                        out=stat[:, st + 1:st + 2], in0=stat[:, st + 1:st + 2],
                        scalar=b.cst[:, C_INV_D:C_INV_D + 1],
                        in1=stat[:, st:st + 1],
                        op0=ALU.mult, op1=ALU.subtract))
                mk[("var", i)] = b.sV.n

                @b.on("scalar")
                def _(scalar, i=i, p=p, st=st):
                    scalar.wait_ge(b.sV.h, mk[("var", i)])
                    scalar.activation(out=stat[:, st + 2:st + 3],
                                      in_=stat[:, st + 1:st + 2], func=AF.Ln,
                                      bias=b.cst[:, C_LNEPS:C_LNEPS + 1])
                    b.sS.inc(scalar.activation(
                        out=stat[:, st + 2:st + 3],
                        in_=stat[:, st + 2:st + 3], func=AF.Exp,
                        scale=b.cst[:, C_NHALF:C_NHALF + 1]))
                mk[("rstd", i)] = b.sS.n

                @b.on("vector")
                def _(vector, i=i, p=p, st=st, own=own):
                    vector.wait_ge(b.sS.h, mk[("rstd", i)])
                    if i >= 2:
                        vector.wait_ge(mk[("scrd", i - 2)][0].h,
                                       mk[("scrd", i - 2)][1])
                    vector.tensor_scalar(
                        out=xnt[p][:], in0=xt[p][:],
                        scalar1=stat[:, st + 3:st + 4],
                        scalar2=stat[:, st + 2:st + 3],
                        op0=ALU.subtract, op1=ALU.mult)
                    b.sV.inc(vector.tensor_copy(xnb[p][:], xnt[p][:]))
                mk[("xfree", i)] = b.sV.n
                mk[("xn", i)] = b.sV.n

                @b.on("scalar")
                def _(scalar, i=i, p=p):
                    scalar.wait_ge(b.sV.h, mk[("xn", i)])
                    b.sS.inc(scalar.activation(out=sq[p][:], in_=xnt[p][:],
                                               func=AF.Square))
                mk[("sq2", i)] = b.sS.n

                @b.on("vector")
                def _(vector, i=i, p=p, st=st):
                    vector.wait_ge(b.sS.h, mk[("sq2", i)])
                    b.sV.inc(vector.reduce_sum(stat[:, st:st + 1], sq[p][:],
                                               axis=AX.X))
                mk[("s2", i)] = b.sV.n

                @b.on("scalar")
                def _(scalar, i=i, st=st):
                    scalar.wait_ge(b.sV.h, mk[("s2", i)])
                    scalar.activation(out=stat[:, st + 1:st + 2],
                                      in_=stat[:, st:st + 1], func=AF.Ln,
                                      scale=b.cst[:, C_INV_D:C_INV_D + 1],
                                      bias=b.cst[:, C_EPS:C_EPS + 1])
                    b.sS.inc(scalar.activation(
                        out=stat[:, st + 1:st + 2],
                        in_=stat[:, st + 1:st + 2], func=AF.Exp,
                        scale=b.cst[:, C_NHALF:C_NHALF + 1]))
                mk[("rstd2", i)] = b.sS.n

                @b.on("vector")
                def _(vector, i=i, p=p, st=st):
                    vector.wait_ge(b.sS.h, mk[("rstd2", i)])
                    vector.tensor_single_scalar(
                        sq[p][:], xnt[p][:], stat[:, st + 1:st + 2],
                        op=ALU.mult)
                    b.sV.inc(vector.tensor_mul(x1b[p][:], sq[p][:],
                                               b.ln1b[:]))
                mk[("x1", i)] = b.sV.n
                mk[("sqfree", i)] = b.sV.n

                @b.on("sync")
                def _(sync, i=i, p=p, tt=tt, own=own, row=row):
                    sync.wait_ge(b.sV.h, mk[("x1", i)])
                    if own:
                        b.sSc[p].dma(sync.dma_start(out=b.XN32[row, :],
                                                    in_=xnt[p][:]))
                        b.sSc[p].dma(sync.dma_start(out=b.XNBF[row, :],
                                                    in_=xnb[p][:]))
                        b.sSc[p].dma(sync.dma_start(out=b.X1BF[row, :],
                                                    in_=x1b[p][:]))
                    else:
                        b.sSc[p].dma(sync.dma_start(out=b.X1BFo[row, :],
                                                    in_=x1b[p][:]))
                mk[("scrd", i)] = (b.sSc[p], b.sSc[p].n)

            @b.on("sync")
            def _(sync):
                for s in (b.sLx[0], b.sLx[1], b.sSc[0], b.sSc[1]):
                    s.wait(sync)

    def emit_p1a(self):
        b = self
        with ExitStack() as px:
            xnF = b.sb("p1_xnF", [128, NT * T], BF16, px)
            cw = b.sb("p1_cw", [128, 64 * 4], F32, px)
            wabb = b.sb("p1_wabb", [128, NT * 64], BF16, px)
            A32t = b.sb("p1_A32t", [128, 512], F32, px)
            B32t = b.sb("p1_B32t", [128, 512], F32, px)
            sl1 = b.sb("p1_sl1", [128, 512], F32, px)
            sl2 = b.sb("p1_sl2", [128, 512], F32, px)
            wt = [b.sb(f"p1_wt{i}", [128, 16 * 128], BF16, px) for i in range(2)]
            qp = [b.sb(f"p1_qp{i}", [128, T], F32, px) for i in range(2)]
            cva = [b.sb(f"p1_cva{i}", [128, T], F32, px) for i in range(2)]
            mxo = [b.sb(f"p1_mxo{i}", [128, T], BF16, px) for i in range(2)]
            prep_px = ExitStack()
            mk = {}

            @b.on("sync")
            def _(sync):
                for dt in range(NT):
                    b.sXF.dma(sync.dma_start(
                        out=xnF[:, dt * T:(dt + 1) * T],
                        in_=b.XNBF[:, dt * 128:(dt + 1) * 128],
                        transpose=True))
                b.sXF.dma(sync.dma_start(
                    out=cw.ap().rearrange("p (ft k) -> p ft k", k=4),
                    in_=b.convw_in.rearrange("(ft p) k -> p ft k", p=128)))
            mk["xnF"] = b.sXF.n

            @b.on("gpsimd")
            def _(gp):
                b.sWab.dma(gp.dma_start(out=wabb[:], in_=b.wab_in))
            mk["wab"] = b.sWab.n

            pab = b.ps("p1_pab", [128, 64], F32, prep_px)
            pc32 = b.ps("p1_pc32", [128, 32], F32, prep_px)
            pct = b.ps("p1_pct", [32, 128], F32, prep_px)
            for tt in range(NT):
                @b.on("tensor")
                def _(tensor, tt=tt):
                    if tt == 0:
                        tensor.wait_ge(b.sXF.h, mk["xnF"])
                        tensor.wait_ge(b.sWab.h, mk["wab"])
                    if tt >= 1:
                        tensor.wait_ge(b.sV.h, mk[("abev", tt - 1)])
                    for dt in range(NT):
                        b.sT.inc(tensor.matmul(
                            pab[:],
                            xnF[:, dt * T + tt * 128:dt * T + (tt + 1) * 128],
                            wabb[:, dt * 64:(dt + 1) * 64],
                            start=(dt == 0), stop=(dt == NT - 1)))
                mk[("abmm", tt)] = b.sT.n

                @b.on("vector")
                def _(vector, tt=tt):
                    vector.wait_ge(b.sT.h, mk[("abmm", tt)])
                    vector.tensor_copy(A32t[:, tt * 32:(tt + 1) * 32],
                                       pab[0:64, 0:32] if False else pab[:, 0:32])
                    b.sV.inc(vector.tensor_copy(B32t[:, tt * 32:(tt + 1) * 32],
                                                pab[:, 32:64]))
                mk[("abev", tt)] = b.sV.n

            @b.on("vector")
            def _(vector):
                b.sV.inc(vector.tensor_add(sl1[:], A32t[:], b.dtb_bc[:]))
            mk["t_a"] = b.sV.n

            @b.on("scalar")
            def _(scalar):
                scalar.wait_ge(b.sV.h, mk["t_a"])
                scalar.activation(out=sl2[:], in_=sl1[:], func=AF.Abs)
                scalar.activation(out=sl2[:], in_=sl2[:], func=AF.Exp, scale=-1.0)
                scalar.activation(out=sl2[:], in_=sl2[:], func=AF.Ln, bias=1.0)
                b.sS.inc(scalar.activation(out=b.nega_bc[:], in_=b.nega_bc[:],
                                           func=AF.Exp))
            mk["t_sp"] = b.sS.n

            @b.on("vector")
            def _(vector):
                vector.wait_ge(b.sS.h, mk["t_sp"])
                vector.tensor_scalar_max(sl1[:], sl1[:], 0.0)
                vector.tensor_add(sl1[:], sl1[:], sl2[:])
                b.sV.inc(vector.scalar_tensor_tensor(
                    out=b.G32[:], in0=b.nega_bc[:], scalar=-1.0, in1=sl1[:],
                    op0=ALU.mult, op1=ALU.mult))
            mk["t_g"] = b.sV.n

            @b.on("scalar")
            def _(scalar):
                scalar.wait_ge(b.sV.h, mk["t_g"])
                scalar.activation(out=b.BET[:], in_=B32t[:], func=AF.Sigmoid)
                b.sS.inc(scalar.activation(out=b.LB[:], in_=b.BET[:], func=AF.Ln))
            mk["t_b"] = b.sS.n

            for tt in range(NT):
                @b.on("tensor")
                def _(tensor, tt=tt):
                    if tt == 0:
                        tensor.wait_ge(b.sV.h, mk["t_g"])
                    if tt >= 1:
                        tensor.wait_ge(b.sV.h, mk[("c32", tt - 1)])
                    b.sT.inc(tensor.matmul(
                        pc32[:], b.tri32[:], b.G32[:, tt * 32:(tt + 1) * 32],
                        start=True, stop=True))
                mk[("cmm", tt)] = b.sT.n

                @b.on("vector")
                def _(vector, tt=tt):
                    vector.wait_ge(b.sT.h, mk[("cmm", tt)])
                    b.sV.inc(vector.tensor_copy(b.C32[:, tt * 32:(tt + 1) * 32],
                                                pc32[:]))
                mk[("c32", tt)] = b.sV.n

                mk[("ctv", tt)] = mk[("c32", tt)]

            @b.on("scalar")
            def _(scalar):
                scalar.wait_ge(b.sV.h, mk[("ctv", NT - 1)])
                b.sS.inc(scalar.activation(out=b.GAM[:], in_=b.C32[:],
                                           func=AF.Exp))
            @b.on("vector")
            def _(vector):
                vector.wait_ge(b.sS.h, b.sS.n)
                b.sV.inc(vector.tensor_add(b.CLB[:], b.C32[:], b.LB[:]))
            mk["t_clb"] = b.sV.n


            prep_px.close()
            pbig = [b.ps(f"p1_pbig{i}", [128, T], F32, px) for i in range(2)]
            prep_v_done = b.sV.n
            prep_s_done = b.sS.n
            NQ, NZ = 64, 32

            def emit_wload(ft):
                p = ft % 2
                src = b.wqkv_in if ft < NQ else b.wz_in
                blk = ft if ft < NQ else ft - NQ

                @b.on("gpsimd")
                def _(gp, ft=ft, p=p, src=src, blk=blk):
                    if ft >= 2:
                        gp.wait_ge(b.sT.h, mk[("fmm", ft - 2)])
                    b.sW[p].dma(gp.dma_start(out=wt[p][:], in_=src[blk]))
                mk[("wld", ft)] = (b.sW[p], b.sW[p].n)

            emit_wload(0)
            emit_wload(1)
            for ft in range(NQ + NZ):
                p = ft % 2
                is_q = ft < NQ

                @b.on("tensor")
                def _(tensor, ft=ft, p=p):
                    tensor.wait_ge(mk[("wld", ft)][0].h, mk[("wld", ft)][1])
                    if ft == 0:
                        tensor.wait_ge(b.sXF.h, mk["xnF"])
                        tensor.wait_ge(b.sV.h, prep_v_done)
                        tensor.wait_ge(b.sS.h, prep_s_done)
                    if ft >= 2:
                        key = ("fcv", ft - 2) if (ft - 2) < NQ else ("fsi", ft - 2)
                        sem = b.sV if (ft - 2) < NQ else b.sS
                        tensor.wait_ge(sem.h, mk[key])
                    for tc in range(4):
                        for dt in range(NT):
                            b.sT.inc(tensor.matmul(
                                pbig[p][:, tc * 512:(tc + 1) * 512],
                                wt[p][:, dt * 128:(dt + 1) * 128],
                                xnF[:, dt * T + tc * 512:dt * T + (tc + 1) * 512],
                                start=(dt == 0), stop=(dt == NT - 1)))
                mk[("fmm", ft)] = b.sT.n
                if ft + 2 < NQ + NZ:
                    emit_wload(ft + 2)

                if is_q:
                    @b.on("vector")
                    def _(vector, ft=ft, p=p):
                        vector.wait_ge(b.sT.h, mk[("fmm", ft)])
                        if ft >= 2:
                            vector.wait_ge(b.sS.h, mk[("fsi", ft - 2)])
                        for tc in range(4):
                            vector.tensor_copy(
                                qp[p][:, tc * 512:(tc + 1) * 512],
                                pbig[p][:, tc * 512:(tc + 1) * 512])
                        c0 = ft * 4
                        vector.tensor_single_scalar(
                            cva[p][:], qp[p][:], cw[:, c0 + 3:c0 + 4],
                            op=ALU.mult)
                        for sh in (1, 2, 3):
                            ins = vector.scalar_tensor_tensor(
                                out=cva[p][:, sh:T],
                                in0=qp[p][:, 0:T - sh],
                                scalar=cw[:, c0 + 3 - sh:c0 + 4 - sh],
                                in1=cva[p][:, sh:T],
                                op0=ALU.mult, op1=ALU.add)
                        b.sV.inc(ins)
                    mk[("fcv", ft)] = b.sV.n

                    @b.on("scalar")
                    def _(scalar, ft=ft, p=p):
                        scalar.wait_ge(b.sV.h, mk[("fcv", ft)])
                        if ft >= 2:
                            scalar.wait_ge(mk[("fst", ft - 2)][0].h,
                                           mk[("fst", ft - 2)][1])
                        b.sS.inc(scalar.activation(out=mxo[p][:], in_=cva[p][:],
                                                   func=AF.Silu))
                    mk[("fsi", ft)] = b.sS.n
                else:
                    @b.on("scalar")
                    def _(scalar, ft=ft, p=p):
                        scalar.wait_ge(b.sT.h, mk[("fmm", ft)])
                        if ft >= 2:
                            scalar.wait_ge(mk[("fst", ft - 2)][0].h,
                                           mk[("fst", ft - 2)][1])
                        b.sS.inc(scalar.activation(out=mxo[p][:], in_=pbig[p][:],
                                                   func=AF.Silu))
                    mk[("fsi", ft)] = b.sS.n

                @b.on("sync")
                def _(sync, ft=ft, p=p, is_q=is_q):
                    sync.wait_ge(b.sS.h, mk[("fsi", ft)])
                    dst = (b.MX[ft * 128:(ft + 1) * 128, :] if is_q
                           else b.ZS[(ft - NQ) * 128:(ft - NQ + 1) * 128, :])
                    b.sMXs[p].dma(sync.dma_start(out=dst, in_=mxo[p][:]))
                mk[("fst", ft)] = (b.sMXs[p], b.sMXs[p].n)

            @b.on("sync")
            def _(sync):
                for s in (b.sMXs[0], b.sMXs[1], b.sXF):
                    s.wait(sync)
            @b.on("gpsimd")
            def _(gp):
                for s in (b.sW[0], b.sW[1], b.sWab):
                    s.wait(gp)

    def emit_p1b(self):
        b = self
        with ExitStack() as px:
            mx = [b.sb("d_mx0", [128, 64 * 128], BF16, px)] * 2
            zsb = [b.sb("d_zs0", [128, 32 * 128], BF16, px)] * 2
            qkn = b.sb("d_qkn", [128, 4096], BF16, px)
            sqs = b.sb("d_sqs", [128, 4096], F32, px)
            ych = b.sb("d_ych", [128, 4096], F32, px)
            ybf = qkn
            rows = b.sb("d_rows", [1, 4096], F32, px)
            clrow = b.sb("d_clrow", [1, 32], F32, px)
            ctn = b.sb("d_ctn", [32, 128], F32, px)
            clbtn = b.sb("d_clbtn", [32, 128], F32, px)
            ctnN = b.sb("d_ctnN", [32, 128], F32, px)
            clbtnN = b.sb("d_clbtnN", [32, 128], F32, px)
            ecx = b.sb("d_ecx", [128, 32], F32, px)
            gcb = b.sb("d_gcb", [128, 32], F32, px)
            gsl = [b.sb(f"d_gsl{i}", [128, 512], F32, px) for i in range(2)]
            egs = [b.sb(f"d_egs{i}", [128, 512], BF16, px) for i in range(2)]
            qgs = [b.sb(f"d_qgs{i}", [128, 512], BF16, px) for i in range(2)]
            esl = [b.sb(f"d_esl{i}", [128, 1536], F32, px) for i in range(2)]
            tb4 = [b.sb("d_tb40", [128, 512], F32, px)] * 2
            ats = [b.sb(f"d_ats{i}", [128, 512], BF16, px) for i in range(2)]
            nns = [[b.sb(f"d_nns{i}_{j}", [128, 512], BF16, px)
                    for j in range(2)] for i in range(2)]
            nps = [[b.sb(f"d_nps{i}_{j}", [128, 512], BF16, px)
                    for j in range(2)] for i in range(2)]
            mps = [[b.sb(f"d_mps{i}_{j}", [128, 512], BF16, px)
                    for j in range(2)] for i in range(2)]
            u14 = tb4
            ubs = [b.sb(f"d_ubs{i}", [128, 512], BF16, px) for i in range(2)]
            dls4 = [b.sb(f"d_dls4{i}", [128, 512], BF16, px) for i in range(2)]
            dss4 = [b.sb(f"d_dss4{i}", [128, 512], BF16, px) for i in range(2)]
            vts2 = [b.sb(f"d_vts2{i}", [128, 512], BF16, px) for i in range(2)]
            ktt = [b.sb(f"d_kt_{g}", [128, 128], BF16, px) for g in range(2)]
            pG = b.ps("d_pG", [128, 512], F32, px)
            phk3 = [[b.ps(f"d_W{w}{i}", [128, 512], F32, px)
                     for w in range(3)] for i in range(2)]
            phk = [phk3[0][0], phk3[0][1], phk3[1][0], phk3[1][1]]
            ptr2 = [b.ps("d_ptr0", [128, 128], BF16, px)] * 2

            mk = {}
            bgate = {k: [] for k in
                     ["G", "W10", "W20", "W30", "W11", "W21", "W31",
                      "h0", "h1", "h2", "h3", "tr0", "tr1"]}

            def bwait(eng, key):
                for s, v in bgate[key]:
                    eng.wait_ge(s.h, v)
                bgate[key] = []

            def bcons(key, sem):
                bgate[key].append((sem, sem.n))

            for n in range(self.p1b_chunks):
                pn = n % 2
                mxs, zss = mx[pn], zsb[pn]

                @b.on("sync")
                def _(sync, n=n, pn=pn, mxs=mxs, zss=zss):
                    if n >= 1:
                        sync.wait_ge(b.sV.h, mk[("mxfree", n - 1)])
                    b.sMxL.dma(sync.dma_start(
                        out=mxs.ap().rearrange("p (ft t) -> p ft t", t=128),
                        in_=b.MX[:, n * 128:(n + 1) * 128].rearrange(
                            "(ft p) t -> p ft t", p=128)))
                    b.sMxL.dma(sync.dma_start(
                        out=zss.ap().rearrange("p (vh t) -> p vh t", t=128),
                        in_=b.ZS[:, n * 128:(n + 1) * 128].rearrange(
                            "(vh p) t -> p vh t", p=128)))
                mk[("mxld", n)] = b.sMxL.n

                @b.on("scalar")
                def _(scalar, n=n, mxs=mxs):
                    scalar.wait_ge(b.sMxL.h, mk[("mxld", n)])
                    if n >= 1:
                        scalar.wait_ge(b.sV.h, mk[("sqsfree", n - 1)])
                    b.sS.inc(scalar.activation(out=sqs[:], in_=mxs[:, 0:4096],
                                               func=AF.Square))
                mk[("sq", n)] = b.sS.n

                for c in range(8):
                    bank = ["W10", "W20", "W11", "W21"][c % 4]
                    pb = phk[c % 4]

                    @b.on("tensor")
                    def _(tensor, c=c, bank=bank, pb=pb, n=n):
                        if c == 0:
                            tensor.wait_ge(b.sS.h, mk[("sq", n)])
                        bwait(tensor, bank)
                        b.sT.inc(tensor.matmul(
                            pb[0:1, 0:512], b.onesc_f[:],
                            sqs[:, c * 512:(c + 1) * 512],
                            start=True, stop=True))
                    mk[("nrm", n, c)] = b.sT.n

                    @b.on("vector")
                    def _(vector, c=c, pb=pb, n=n):
                        vector.wait_ge(b.sT.h, mk[("nrm", n, c)])
                        b.sV.inc(vector.tensor_copy(
                            rows[0:1, c * 512:(c + 1) * 512], pb[0:1, 0:512]))
                    bcons(bank, b.sV)
                mk[("nrmv", n)] = b.sV.n

                @b.on("scalar")
                def _(scalar, n=n):
                    scalar.wait_ge(b.sV.h, mk[("nrmv", n)])
                    scalar.activation(out=rows[:], in_=rows[:], func=AF.Ln,
                                      bias=b.cst[0:1, C_EPS:C_EPS + 1])
                    b.sS.inc(scalar.activation(
                        out=rows[:], in_=rows[:], func=AF.Exp,
                        scale=b.cst[0:1, C_NHALF:C_NHALF + 1]))
                mk[("scl", n)] = b.sS.n

                @b.on("vector")
                def _(vector, n=n):
                    vector.wait_ge(b.sS.h, mk[("scl", n)])
                    b.sV.inc(vector.tensor_single_scalar(
                        rows[0:1, 0:2048], rows[0:1, 0:2048],
                        b.cst[0:1, C_QSCL:C_QSCL + 1], op=ALU.mult))
                mk[("sclq", n)] = b.sV.n

                for c in range(8):
                    bank = ["W10", "W20", "W11", "W21"][c % 4]
                    pb = phk[c % 4]

                    @b.on("tensor")
                    def _(tensor, c=c, bank=bank, pb=pb, n=n):
                        if c == 0:
                            tensor.wait_ge(b.sV.h, mk[("sclq", n)])
                        bwait(tensor, bank)
                        b.sT.inc(tensor.matmul(
                            pb[:], b.onesr_f[:],
                            rows[0:1, c * 512:(c + 1) * 512],
                            start=True, stop=True))
                    mk[("qbc", n, c)] = b.sT.n

                    @b.on("vector")
                    def _(vector, c=c, pb=pb, n=n, mxs=mxs):
                        vector.wait_ge(b.sT.h, mk[("qbc", n, c)])
                        if c == 0 and n >= 1:
                            vector.wait_ge(b.sYst.h, mk[("yst", n - 1)])
                        b.sV.inc(vector.tensor_mul(
                            qkn[:, c * 512:(c + 1) * 512],
                            mxs[:, c * 512:(c + 1) * 512], pb[:]))
                    bcons(bank, b.sV)
                mk[("qkn", n)] = b.sV.n
                if n == 0 and "DT" in b.dbg:
                    @b.on("sync")
                    def _(sync, n=n):
                        sync.wait_ge(b.sV.h, mk[("qkn", n)])
                        b.sDBG.dma(sync.dma_start(out=b.DBG_QKN, in_=qkn[:]))

                @b.on("tensor")
                def _(tensor, n=n):
                    bwait(tensor, "G")
                    b.sT.inc(tensor.transpose(
                        pG[0:32, 0:128], b.C32[:, n * 32:(n + 1) * 32],
                        b.ident32[:]))
                mk[("ctt", n)] = b.sT.n

                @b.on("vector")
                def _(vector, n=n):
                    vector.wait_ge(b.sT.h, mk[("ctt", n)])
                    vector.tensor_copy(ctn[:], pG[0:32, 0:128])
                    b.sV.inc(vector.tensor_single_scalar(
                        ctnN[:], pG[0:32, 0:128], -1.0, op=ALU.mult))
                bcons("G", b.sV)
                mk[("ctn", n)] = b.sV.n

                @b.on("tensor")
                def _(tensor, n=n):
                    bwait(tensor, "G")
                    b.sT.inc(tensor.transpose(
                        pG[0:32, 0:128], b.CLB[:, n * 32:(n + 1) * 32],
                        b.ident32[:]))
                mk[("clbt", n)] = b.sT.n

                @b.on("vector")
                def _(vector, n=n):
                    vector.wait_ge(b.sT.h, mk[("clbt", n)])
                    vector.tensor_copy(clbtn[:], pG[0:32, 0:128])
                    b.sV.inc(vector.tensor_single_scalar(
                        clbtnN[:], pG[0:32, 0:128], -1.0, op=ALU.mult))
                bcons("G", b.sV)
                mk[("clbtn", n)] = b.sV.n

                @b.on("tensor")
                def _(tensor, n=n):
                    bwait(tensor, "G")
                    b.sT.inc(tensor.matmul(
                        pG[0:1, 0:32], b.e127[:],
                        b.C32[:, n * 32:(n + 1) * 32], start=True, stop=True))
                mk[("clr", n)] = b.sT.n

                @b.on("vector")
                def _(vector, n=n):
                    vector.wait_ge(b.sT.h, mk[("clr", n)])
                    b.sV.inc(vector.tensor_copy(clrow[:], pG[0:1, 0:32]))
                bcons("G", b.sV)
                mk[("clrv", n)] = b.sV.n

                @b.on("tensor")
                def _(tensor, n=n):
                    bwait(tensor, "G")
                    b.sT.inc(tensor.matmul(pG[0:128, 0:32], b.onesr_f[:],
                                           clrow[:], start=True, stop=True))
                mk[("ecb", n)] = b.sT.n

                @b.on("vector")
                def _(vector, n=n):
                    vector.wait_ge(b.sT.h, mk[("ecb", n)])
                    b.sV.inc(vector.tensor_sub(
                        ecx[:], pG[0:128, 0:32],
                        b.C32[:, n * 32:(n + 1) * 32]))
                mk[("ecd", n)] = b.sV.n

                @b.on("scalar")
                def _(scalar, n=n):
                    scalar.wait_ge(b.sV.h, mk[("ecd", n)])
                    scalar.activation(out=ecx[:], in_=ecx[:], func=AF.Exp)
                    b.sS.inc(scalar.activation(out=gcb[:], in_=pG[0:128, 0:32],
                                               func=AF.Exp))
                bcons("G", b.sS)
                mk[("ecg", n)] = b.sS.n

                def emit_tr(src_ap, dst_ap, slot, n=n):
                    trk = f"tr{slot}"

                    @b.on("tensor")
                    def _(tensor):
                        bwait(tensor, trk)
                        b.sT.inc(tensor.transpose(ptr2[slot][:], src_ap,
                                                  b.ident[:]))
                    tm = b.sT.n

                    @b.on("vector")
                    def _(vector):
                        vector.wait_ge(b.sT.h, tm)
                        b.sV.inc(vector.tensor_copy(dst_ap, ptr2[slot][:]))
                    bcons(trk, b.sV)
                    return b.sV.n

                pending_groups = []
                for kp in range(self.p1b_pairs):
                    s = kp % 2
                    W1, W2, W3 = phk3[s]
                    kh0, kh1 = 2 * kp, 2 * kp + 1
                    vhs = [4 * kp + g for g in range(4)]
                    sl = lambda g: slice(g * 128, (g + 1) * 128)

                    @b.on("tensor")
                    def _(tensor, n=n, kp=kp, kh0=kh0, kh1=kh1):
                        if kp == 0:
                            tensor.wait_ge(b.sV.h, mk[("qkn", n)])
                        bwait(tensor, "G")
                        for j, kh in enumerate((kh0, kh1)):
                            Kn = qkn[:, 2048 + kh * 128:2048 + (kh + 1) * 128]
                            Qn = qkn[:, kh * 128:(kh + 1) * 128]
                            b.sT.inc(tensor.matmul(
                                pG[:, j * 256:j * 256 + 128], Kn, Kn,
                                start=True, stop=True))
                            b.sT.inc(tensor.matmul(
                                pG[:, j * 256 + 128:j * 256 + 256], Kn, Qn,
                                start=True, stop=True))
                    mk[("gram", n, kp)] = b.sT.n

                    @b.on("vector")
                    def _(vector, n=n, kp=kp, s=s):
                        vector.wait_ge(b.sT.h, mk[("gram", n, kp)])
                        if ("gsl", n, kp - 2) in mk:
                            vector.wait_ge(b.sV.h, mk[("gslfree", n, kp - 2)])
                        b.sV.inc(vector.tensor_copy(gsl[s][:], pG[:]))
                    bcons("G", b.sV)
                    mk[("gsl", n, kp)] = b.sV.n

                    for j, kh in enumerate((kh0, kh1)):
                        Kn = qkn[:, 2048 + kh * 128:2048 + (kh + 1) * 128]
                        mk[("ktt", n, kh)] = (b.sV.n if self.p1b_notr
                                              else emit_tr(Kn, ktt[j][:], 0))
                    for g, vh in enumerate(vhs):
                        mxv = mxs[:, (32 + vh) * 128:(33 + vh) * 128]
                        mk[("vts", n, vh)] = (b.sV.n if self.p1b_notr
                                              else emit_tr(mxv, vts2[s][:, sl(g)], 0))

                    def group_gen(n=n, kp=kp, s=s, kh0=kh0, kh1=kh1, vhs=vhs):
                        W1, W2, W3 = phk3[s]
                        sl = lambda g: slice(g * 128, (g + 1) * 128)
                        cA = b.C32[:, n * 32 + vhs[0]:n * 32 + vhs[0] + 4]
                        cc = [b.C32[:, n * 32 + vh:n * 32 + vh + 1] for vh in vhs]
                        clbc = [b.CLB[:, n * 32 + vh:n * 32 + vh + 1] for vh in vhs]
                        gamc = [b.GAM[:, n * 32 + vh:n * 32 + vh + 1] for vh in vhs]
                        betc = [b.BET[:, n * 32 + vh:n * 32 + vh + 1] for vh in vhs]
                        ecc = [ecx[:, vh:vh + 1] for vh in vhs]
                        gcbc = [gcb[:, vh:vh + 1] for vh in vhs]
                        s0sl = b.S0b[:, vhs[0] * 128:(vhs[0] + 4) * 128]
                        s32l = b.S32[:, vhs[0] * 128:(vhs[0] + 4) * 128]
                        zsl = zss[:, vhs[0] * 128:(vhs[0] + 4) * 128]
                        gt = gsl[s]
                        def grep(off, j):
                            ap = gt[:, off + j * 256:off + j * 256 + 1]
                            return bass.AP(tensor=ap.tensor, offset=ap.offset,
                                           ap=[[512, 128], [0, 2], [1, 128]])
                        def v3(t, j):
                            ap = t[:] if not isinstance(t, bass.AP) else t
                            ps = ap.tensor.shape[-1]
                            return bass.AP(tensor=ap.tensor,
                                           offset=ap.offset + j * 256,
                                           ap=[[ps, 128], [128, 2], [1, 128]])
                        def qrep(j):
                            ap = qkn[:, (kh0 + j) * 128:(kh0 + j) * 128 + 1]
                            return bass.AP(tensor=ap.tensor, offset=ap.offset,
                                           ap=[[4096, 128], [0, 2], [1, 128]])

                        def f(tensor):
                            bwait(tensor, f"W1{s}")
                            tensor.wait_ge(b.sV.h, mk[("ctn", n)])
                            tensor.wait_ge(b.sV.h, mk[("clbtn", n)])
                            for g, vh in enumerate(vhs):
                                b.sT.inc(tensor.matmul(
                                    W1[:, sl(g)],
                                    b.sel[:, vh * 128:(vh + 1) * 128],
                                    ctn[:], start=True, stop=True))
                        yield ("tensor", f)
                        m_cf = b.sT.n

                        def f(scalar):
                            scalar.wait_ge(b.sT.h, m_cf)
                            if ("egfree", s) in mk:
                                scalar.wait_ge(b.sV.h, mk[("egfree", s)])
                            b.sS.inc(scalar.activation(out=egs[s][:], in_=W1[:],
                                                       func=AF.Exp))
                        yield ("scalar", f)
                        m_eg = b.sS.n
                        bcons(f"W1{s}", b.sS)

                        def f(vector):
                            vector.wait_ge(b.sS.h, m_eg)
                            if ("qgfree", s) in mk:
                                vector.wait_ge(b.sT.h, mk[("qgfree", s)])
                            vector.tensor_tensor(
                                v3(qgs[s], 0), qrep(0), v3(egs[s], 0),
                                op=ALU.mult)
                            b.sV.inc(vector.tensor_tensor(
                                v3(qgs[s], 1), qrep(1), v3(egs[s], 1),
                                op=ALU.mult))
                        yield ("vector", f)
                        m_qg = b.sV.n
                        mk[("egfree", s)] = m_qg

                        def f(tensor):
                            bwait(tensor, f"W1{s}")
                            bwait(tensor, f"W2{s}")
                            bwait(tensor, f"W3{s}")
                            for g, vh in enumerate(vhs):
                                svh = b.sel[:, vh * 128:(vh + 1) * 128]
                                b.sT.inc(tensor.matmul(W1[:, sl(g)], svh,
                                                       ctn[:], start=True,
                                                       stop=False))
                                b.sT.inc(tensor.matmul(W1[:, sl(g)], ctnN[:],
                                                       svh, start=False,
                                                       stop=True))
                                b.sT.inc(tensor.matmul(W2[:, sl(g)], svh,
                                                       clbtn[:], start=True,
                                                       stop=False))
                                b.sT.inc(tensor.matmul(W2[:, sl(g)], ctnN[:],
                                                       svh, start=False,
                                                       stop=True))
                                b.sT.inc(tensor.matmul(W3[:, sl(g)], svh,
                                                       ctn[:], start=True,
                                                       stop=False))
                                b.sT.inc(tensor.matmul(W3[:, sl(g)], clbtnN[:],
                                                       svh, start=False,
                                                       stop=True))
                        yield ("tensor", f)
                        m_dmm = b.sT.n

                        def f(vector):
                            vector.wait_ge(b.sT.h, m_dmm)
                            if ("esfree", s) in mk:
                                vector.wait_ge(b.sS.h, mk[("esfree", s)])
                            vector.tensor_single_scalar(
                                esl[s][:, 0:512], W1[:], 0.0, op=ALU.min)
                            vector.tensor_single_scalar(
                                esl[s][:, 512:1024], W2[:], 0.0, op=ALU.min)
                            b.sV.inc(vector.tensor_single_scalar(
                                esl[s][:, 1024:1536], W3[:], 0.0, op=ALU.max))
                        yield ("vector", f)
                        m_dcl = b.sV.n
                        bcons(f"W1{s}", b.sV)
                        bcons(f"W2{s}", b.sV)
                        bcons(f"W3{s}", b.sV)

                        def f(scalar):
                            scalar.wait_ge(b.sV.h, m_dcl)
                            scalar.activation(out=esl[s][:, 0:1024],
                                              in_=esl[s][:, 0:1024],
                                              func=AF.Exp)
                            b.sS.inc(scalar.activation(
                                out=esl[s][:, 1024:1536],
                                in_=esl[s][:, 1024:1536], func=AF.Exp,
                                scale=-1.0))
                        yield ("scalar", f)
                        m_exp = b.sS.n
                        mk[("esfree", s)] = m_exp

                        def f(vector):
                            vector.wait_ge(b.sS.h, m_exp)
                            vector.wait_ge(b.sV.h, mk[("gsl", n, kp)])
                            if ("bldfree", s) in mk:
                                for sm, vv in mk[("bldfree", s)]:
                                    vector.wait_ge(sm.h, vv)
                            for j in range(2):
                                vector.scalar_tensor_tensor(
                                    out=v3(tb4[s], j), in0=grep(128, j),
                                    scalar=1.0, in1=v3(esl[s][:, 0:512], j),
                                    op0=ALU.mult, op1=ALU.mult)
                            vector.tensor_mul(ats[s][:], tb4[s][:],
                                              b.mskui4[:])
                            for j in range(2):
                                vector.scalar_tensor_tensor(
                                    out=v3(tb4[s], j), in0=grep(0, j),
                                    scalar=-1.0,
                                    in1=v3(esl[s][:, 512:1024], j),
                                    op0=ALU.mult, op1=ALU.mult)
                            vector.tensor_mul(nps[s][0][:], tb4[s][:],
                                              b.mskus4[:])
                            for j in range(2):
                                vector.scalar_tensor_tensor(
                                    out=v3(tb4[s], j), in0=grep(0, j),
                                    scalar=-1.0,
                                    in1=v3(esl[s][:, 1024:1536], j),
                                    op0=ALU.mult, op1=ALU.mult)
                            vector.tensor_mul(nns[s][0][:], tb4[s][:],
                                              b.mskls4[:])
                            b.sV.inc(vector.tensor_add(mps[s][0][:],
                                                       b.id4[:],
                                                       nps[s][0][:]))
                        yield ("vector", f)
                        m_bld = b.sV.n
                        mk[("gslfree", n, kp)] = m_bld

                        def f(tensor):
                            bwait(tensor, f"W1{s}")
                            if n >= 1:
                                for vh in vhs:
                                    sm, vv = mk[("s0w", n - 1, vh)]
                                    tensor.wait_ge(sm.h, vv)
                            for g, (vh, kh) in enumerate(
                                    zip(vhs, (kh0, kh0, kh1, kh1))):
                                Kn = qkn[:, 2048 + kh * 128:
                                         2048 + (kh + 1) * 128]
                                b.sT.inc(tensor.matmul(
                                    W1[:, sl(g)], Kn,
                                    b.S0b[:, vh * 128:(vh + 1) * 128],
                                    start=True, stop=True))
                        yield ("tensor", f)
                        m_w0 = b.sT.n

                        def f(vector):
                            vector.wait_ge(b.sT.h, m_w0)
                            for vh in vhs:
                                vector.wait_ge(b.sV.h, mk[("vts", n, vh)])
                            if ("ubfree", s) in mk:
                                vector.wait_ge(b.sT.h, mk[("ubfree", s)])
                            for g in range(4):
                                vector.tensor_scalar(
                                    out=u14[s][:, sl(g)], in0=W1[:, sl(g)],
                                    scalar1=gamc[g], scalar2=betc[g],
                                    op0=ALU.mult, op1=ALU.mult)
                            for g in range(4):
                                ins = vector.scalar_tensor_tensor(
                                    out=ubs[s][:, sl(g)],
                                    in0=vts2[s][:, sl(g)], scalar=betc[g],
                                    in1=u14[s][:, sl(g)], op0=ALU.mult,
                                    op1=ALU.subtract)
                            b.sV.inc(ins)
                        yield ("vector", f)
                        m_ub = b.sV.n
                        bcons(f"W1{s}", b.sV)

                        cur = 0
                        m_nev = (m_dcl, b.sV)
                        m_pev = (m_exp, b.sS)
                        m_madd = (m_ub, b.sV)
                        for it in range(1, 7):
                            nxt = 1 - cur

                            def f(tensor, cur=cur, it=it, m_nev=m_nev,
                                  m_pev=m_pev, m_madd=m_madd):
                                tensor.wait_ge(m_nev[1].h, m_nev[0])
                                tensor.wait_ge(m_pev[1].h, m_pev[0])
                                if it == 1:
                                    tensor.wait_ge(b.sV.h, m_bld)
                                    tensor.wait_ge(b.sV.h, m_ub)
                                else:
                                    tensor.wait_ge(m_madd[1].h, m_madd[0])
                                for g in range(4):
                                    b.sT.inc(tensor.matmul(
                                        W1[:, sl(g)], nps[s][cur][:, sl(g)],
                                        nns[s][cur][:, sl(g)],
                                        start=True, stop=True))
                                if it < 6:
                                    for g in range(4):
                                        b.sT.inc(tensor.matmul(
                                            W2[:, sl(g)],
                                            nns[s][cur][:, sl(g)],
                                            nps[s][cur][:, sl(g)],
                                            start=True, stop=True))
                            yield ("tensor", f)
                            m_sq = b.sT.n

                            def f(vector, nxt=nxt, m_sq=m_sq):
                                vector.wait_ge(b.sT.h, m_sq)
                                b.sV.inc(vector.tensor_copy(nns[s][nxt][:],
                                                            W1[:]))
                            yield ("vector", f)
                            m_nev = (b.sV.n, b.sV)

                            if it < 6:
                                def f(scalar, nxt=nxt, m_sq=m_sq):
                                    scalar.wait_ge(b.sT.h, m_sq)
                                    b.sS.inc(scalar.activation(
                                        out=nps[s][nxt][:], in_=W2[:],
                                        func=AF.Copy))
                                yield ("scalar", f)
                                m_pev = (b.sS.n, b.sS)

                            def f(tensor, nxt=nxt, cur=cur, m_nev=m_nev,
                                  m_madd=m_madd):
                                tensor.wait_ge(m_nev[1].h, m_nev[0])
                                tensor.wait_ge(m_madd[1].h, m_madd[0])
                                for g in range(4):
                                    b.sT.inc(tensor.matmul(
                                        W3[:, sl(g)], nns[s][nxt][:, sl(g)],
                                        mps[s][cur][:, sl(g)],
                                        start=True, stop=True))
                            yield ("tensor", f)
                            m_mm = b.sT.n

                            def f(vector, nxt=nxt, cur=cur, m_mm=m_mm):
                                vector.wait_ge(b.sT.h, m_mm)
                                b.sV.inc(vector.tensor_add(
                                    mps[s][nxt][:], W3[:], mps[s][cur][:]))
                            yield ("vector", f)
                            m_madd = (b.sV.n, b.sV)
                            cur = nxt
                        m_inv = m_madd[0]

                        def f(tensor, cur=cur):
                            tensor.wait_ge(b.sV.h, m_inv)
                            tensor.wait_ge(m_nev[1].h, m_nev[0])
                            for g in range(4):
                                b.sT.inc(tensor.matmul(
                                    W1[:, sl(g)], mps[s][cur][:, sl(g)],
                                    ubs[s][:, sl(g)], start=True, stop=True))
                        yield ("tensor", f)
                        m_del = b.sT.n

                        def f(vector):
                            vector.wait_ge(b.sT.h, m_del)
                            if ("dlfree", s) in mk:
                                vector.wait_ge(b.sT.h, mk[("dlfree", s)])
                            b.sV.inc(vector.tensor_copy(dls4[s][:], W1[:]))
                        yield ("vector", f)
                        m_dlb = b.sV.n

                        def f(scalar):
                            scalar.wait_ge(b.sT.h, m_del)
                            scalar.wait_ge(b.sS.h, mk[("ecg", n)])
                            if ("dsfree", s) in mk:
                                scalar.wait_ge(b.sT.h, mk[("dsfree", s)])
                            for g in range(4):
                                ins = scalar.activation(
                                    out=dss4[s][:, sl(g)], in_=W1[:, sl(g)],
                                    func=AF.Copy, scale=ecc[g])
                            b.sS.inc(ins)
                        yield ("scalar", f)
                        m_dls = b.sS.n
                        bgate[f"W1{s}"] = [(b.sV, m_dlb), (b.sS, m_dls)]

                        def f(tensor, m_pev=m_pev):
                            bwait(tensor, f"W2{s}")
                            tensor.wait_ge(b.sV.h, m_dlb)
                            tensor.wait_ge(b.sV.h, m_qg)
                            tensor.wait_ge(m_pev[1].h, m_pev[0])
                            for g in range(4):
                                b.sT.inc(tensor.matmul(
                                    W2[:, sl(g)], dls4[s][:, sl(g)],
                                    ats[s][:, sl(g)], start=True, stop=False))
                                b.sT.inc(tensor.matmul(
                                    W2[:, sl(g)],
                                    b.S0b[:, vhs[g] * 128:(vhs[g] + 1) * 128],
                                    qgs[s][:, sl(g)], start=False, stop=True))
                        yield ("tensor", f)
                        m_omm = b.sT.n
                        mk[("qgfree", s)] = m_omm
                        mk[("dlfree", s)] = m_omm

                        def f(vector):
                            vector.wait_ge(b.sT.h, m_omm)
                            b.sV.inc(vector.tensor_mul(
                                ych[:, vhs[0] * 128:(vhs[0] + 4) * 128],
                                W2[:], zsl))
                        yield ("vector", f)
                        m_y = b.sV.n
                        bgate[f"W2{s}"] = [(b.sV, m_y)]

                        def f(tensor):
                            bwait(tensor, f"W3{s}")
                            tensor.wait_ge(b.sS.h, m_dls)
                            tensor.wait_ge(b.sV.h, mk[("ktt", n, kh0)])
                            tensor.wait_ge(b.sV.h, mk[("ktt", n, kh1)])
                            for g, jj in enumerate((0, 0, 1, 1)):
                                b.sT.inc(tensor.matmul(
                                    W3[:, sl(g)], ktt[jj][:],
                                    dss4[s][:, sl(g)], start=True, stop=True))
                        yield ("tensor", f)
                        m_smm = b.sT.n
                        mk[("dsfree", s)] = m_smm

                        def f(vector):
                            vector.wait_ge(b.sT.h, m_smm)
                            for g in range(4):
                                vector.scalar_tensor_tensor(
                                    out=s32l[:, sl(g)], in0=s32l[:, sl(g)],
                                    scalar=gcbc[g], in1=W3[:, sl(g)],
                                    op0=ALU.mult, op1=ALU.add)
                            b.sV.inc(vector.tensor_copy(s0sl, s32l))
                        yield ("vector", f)
                        for vh in vhs:
                            mk[("s0w", n, vh)] = (b.sV, b.sV.n)
                        mk[("ubfree", s)] = m_del
                        mk[("bldfree", s)] = [(b.sT, m_smm)]
                        bgate[f"W3{s}"] = [(b.sV, b.sV.n)]

                    pending_groups.append(group_gen())
                    if kp % 2 == 1 or kp == self.p1b_pairs - 1:
                        done = [False] * len(pending_groups)
                        while not all(done):
                            for gi, gg in enumerate(pending_groups):
                                if done[gi]:
                                    continue
                                try:
                                    engname, fn = next(gg)
                                except StopIteration:
                                    done[gi] = True
                                    continue
                                b.on(engname)(fn)
                        pending_groups = []

                mk[("mxfree", n)] = b.sV.n


                @b.on("scalar")
                def _(scalar, n=n):
                    scalar.wait_ge(b.sV.h, mk[("mxfree", n)])
                    b.sS.inc(scalar.activation(out=sqs[:], in_=ych[:],
                                               func=AF.Square))
                mk[("ysq", n)] = b.sS.n

                for c in range(8):
                    bank = ["W10", "W20", "W11", "W21"][c % 4]
                    pb = phk[c % 4]

                    @b.on("tensor")
                    def _(tensor, c=c, bank=bank, pb=pb, n=n):
                        if c == 0:
                            tensor.wait_ge(b.sS.h, mk[("ysq", n)])
                        bwait(tensor, bank)
                        b.sT.inc(tensor.matmul(
                            pb[0:1, 0:512], b.onesc_f[:],
                            sqs[:, c * 512:(c + 1) * 512],
                            start=True, stop=True))
                    mk[("ynrm", n, c)] = b.sT.n

                    @b.on("vector")
                    def _(vector, c=c, pb=pb, n=n):
                        vector.wait_ge(b.sT.h, mk[("ynrm", n, c)])
                        b.sV.inc(vector.tensor_copy(
                            rows[0:1, c * 512:(c + 1) * 512], pb[0:1, 0:512]))
                    bcons(bank, b.sV)
                mk[("ynrmv", n)] = b.sV.n

                @b.on("scalar")
                def _(scalar, n=n):
                    scalar.wait_ge(b.sV.h, mk[("ynrmv", n)])
                    scalar.activation(out=rows[:], in_=rows[:], func=AF.Ln,
                                      scale=b.cst[0:1, C_INV_DV:C_INV_DV + 1],
                                      bias=b.cst[0:1, C_EPS:C_EPS + 1])
                    b.sS.inc(scalar.activation(
                        out=rows[:], in_=rows[:], func=AF.Exp,
                        scale=b.cst[0:1, C_NHALF:C_NHALF + 1]))
                mk[("yscl", n)] = b.sS.n

                for c in range(8):
                    bank = ["W10", "W20", "W11", "W21"][c % 4]
                    pb = phk[c % 4]

                    @b.on("tensor")
                    def _(tensor, c=c, bank=bank, pb=pb, n=n):
                        if c == 0:
                            tensor.wait_ge(b.sS.h, mk[("yscl", n)])
                        bwait(tensor, bank)
                        b.sT.inc(tensor.matmul(
                            pb[:], b.onesr_f[:],
                            rows[0:1, c * 512:(c + 1) * 512],
                            start=True, stop=True))
                    mk[("ybc", n, c)] = b.sT.n

                    @b.on("vector")
                    def _(vector, c=c, pb=pb, n=n):
                        vector.wait_ge(b.sT.h, mk[("ybc", n, c)])
                        if n >= 1:
                            vector.wait_ge(b.sYst.h, mk[("yst", n - 1)])
                        b.sV.inc(vector.scalar_tensor_tensor(
                            out=ybf[:, c * 512:(c + 1) * 512],
                            in0=ych[:, c * 512:(c + 1) * 512],
                            scalar=b.normw[:], in1=pb[:],
                            op0=ALU.mult, op1=ALU.mult))
                    bcons(bank, b.sV)
                mk[("ybf", n)] = b.sV.n
                mk[("sqsfree", n)] = b.sV.n

                @b.on("sync")
                def _(sync, n=n):
                    sync.wait_ge(b.sV.h, mk[("ybf", n)])
                    b.sYst.dma(sync.dma_start(
                        out=b.YS[:, n * HV * 128:(n + 1) * HV * 128],
                        in_=ybf[:]))
                mk[("yst", n)] = b.sYst.n

            @b.on("sync")
            def _(sync):
                b.sMxL.wait(sync)
                b.sYst.wait(sync)
                b.sDBG.wait(sync)

    def emit_p1c(self):
        b = self
        with ExitStack() as px:
            wob = [b.sb(f"c_wob{i}", [128, 32 * 512], BF16, px) for i in range(2)]
            ysl = [b.sb(f"c_ysl{i}", [128, 32 * 128], BF16, px) for i in range(2)]
            xst = [b.sb(f"c_xst{i}", [128, 512], F32, px) for i in range(2)]
            r1t = [b.sb(f"c_r1t{i}", [128, 512], F32, px) for i in range(2)]
            pc = [b.ps(f"c_pc{i}", [128, 512], F32, px) for i in range(2)]
            mk = {}
            for dc in range(4):
                pw = dc % 2

                @b.on("gpsimd")
                def _(gp, dc=dc, pw=pw):
                    if dc >= 2:
                        gp.wait_ge(b.sT.h, mk[("cmm", dc - 2, NT - 1)])
                    b.sWob[pw].dma(gp.dma_start(out=wob[pw][:],
                                                 in_=b.wout_in[dc]))
                mk[("wld", dc)] = (b.sWob[pw], b.sWob[pw].n)

                for tc in range(NT):
                    pp = tc % 2
                    it = dc * NT + tc

                    @b.on("sync")
                    def _(sync, dc=dc, tc=tc, pp=pp, it=it):
                        if it >= 2:
                            sync.wait_ge(b.sT.h, mk[("cmm", *divmod(it - 2, NT))])
                        b.sYld[pp].dma(sync.dma_start(
                            out=ysl[pp][:],
                            in_=b.YS[:, tc * HV * 128:
                                     (tc + 1) * HV * 128]))
                        b.sXld[pp].dma(sync.dma_start(
                            out=xst[pp][:],
                            in_=b.x_own[tc * 128:(tc + 1) * 128,
                                        dc * 512:(dc + 1) * 512]))
                    mk[("yld", it)] = (b.sYld[pp], b.sYld[pp].n,
                                       b.sXld[pp], b.sXld[pp].n)

                    @b.on("tensor")
                    def _(tensor, dc=dc, tc=tc, pp=pp, it=it, pw=pw):
                        s1, v1, s2, v2 = mk[("yld", it)]
                        tensor.wait_ge(s1.h, v1)
                        tensor.wait_ge(mk[("wld", dc)][0].h, mk[("wld", dc)][1])
                        if it >= 2:
                            tensor.wait_ge(b.sV.h, mk[("cev", it - 2)])
                        for vh in range(HV):
                            b.sT.inc(tensor.matmul(
                                pc[pp][:],
                                ysl[pp][:, vh * 128:(vh + 1) * 128],
                                wob[pw][:, vh * 512:(vh + 1) * 512],
                                start=(vh == 0), stop=(vh == HV - 1)))
                    mk[("cmm", dc, tc)] = b.sT.n

                    @b.on("vector")
                    def _(vector, dc=dc, tc=tc, pp=pp, it=it):
                        vector.wait_ge(b.sT.h, mk[("cmm", dc, tc)])
                        s1, v1, s2, v2 = mk[("yld", it)]
                        vector.wait_ge(s2.h, v2)
                        if it >= 2:
                            vector.wait_ge(mk[("r1st", it - 2)][0].h,
                                           mk[("r1st", it - 2)][1])
                        b.sV.inc(vector.tensor_add(r1t[pp][:], pc[pp][:],
                                                   xst[pp][:]))
                    mk[("cev", it)] = b.sV.n

                    @b.on("sync")
                    def _(sync, dc=dc, tc=tc, pp=pp, it=it):
                        sync.wait_ge(b.sV.h, mk[("cev", it)])
                        b.sR1[pp].dma(sync.dma_start(
                            out=b.R1[tc * 128:(tc + 1) * 128,
                                     dc * 512:(dc + 1) * 512],
                            in_=r1t[pp][:]))
                    mk[("r1st", it)] = (b.sR1[pp], b.sR1[pp].n)

            @b.on("sync")
            def _(sync):
                for s in (b.sYld[0], b.sYld[1], b.sXld[0], b.sXld[1],
                          b.sR1[0], b.sR1[1]):
                    s.wait(sync)
            @b.on("gpsimd")
            def _(gp):
                b.sWob[0].wait(gp)
                b.sWob[1].wait(gp)

    def emit_f1(self):
        b = self
        with ExitStack() as px:
            x1F = b.sb("f1_x1F", [128, NT * T], BF16, px)
            kwt = [b.sb(f"f1_kw{i}", [128, NT * 128], BF16, px) for i in range(2)]
            vwt = [b.sb(f"f1_vw{i}", [128, NT * 128], BF16, px) for i in range(2)]
            qwt = [b.sb(f"f1_qw{i}", [128, NT * 128], BF16, px) for i in range(2)]
            kbuf = [b.sb(f"f1_kb{i}", [128, T], BF16, px) for i in range(2)]
            vbuf = [b.sb(f"f1_vb{i}", [128, T], BF16, px) for i in range(2)]
            qbuf = [b.sb(f"f1_qb{i}", [128, T], BF16, px) for i in range(2)]
            ksq = b.sb("f1_ksq", [128, T], F32, px)
            krow = b.sb("f1_krow", [1, T], F32, px)
            pk = [b.ps(f"f1_pk{i}", [128, 512], F32, px) for i in range(2)]
            pv = [b.ps(f"f1_pv{i}", [128, 128], F32, px) for i in range(2)]
            pr = b.ps("f1_pr", [1, 512], F32, px)
            mk = {}
            sX1 = b.sem("f1_sX1")
            sKW = [b.sem("f1_sKW0"), b.sem("f1_sKW1")]
            sST = [b.sem("f1_sST0"), b.sem("f1_sST1")]

            def TV(key_t, key_v, emit_t, emit_v, waits_t=()):
                @b.on("tensor")
                def _(tensor):
                    for s, v in waits_t:
                        tensor.wait_ge(s.h if hasattr(s, "h") else s, v)
                    emit_t(tensor)
                mk[key_t] = b.sT.n

                @b.on("vector")
                def _(vector):
                    vector.wait_ge(b.sT.h, mk[key_t])
                    emit_v(vector)
                mk[key_v] = b.sV.n

            for half in range(2):
                src_x1 = b.X1BFo if half == 0 else b.X1BF

                @b.on("sync")
                def _(sync, half=half, src_x1=src_x1):
                    if half == 1:
                        sync.wait_ge(b.sT.h, mk[("allmm", 0)])
                    for dt in range(NT):
                        sX1.dma(sync.dma_start(
                            out=x1F[:, dt * T:(dt + 1) * T],
                            in_=src_x1[:, dt * 128:(dt + 1) * 128],
                            transpose=True))
                mk[("x1", half)] = sX1.n

                for h in range(NH):
                    hp = h % 2
                    it = half * NH + h

                    @b.on("gpsimd")
                    def _(gp, h=h, hp=hp, it=it):
                        if it >= 2:
                            gp.wait_ge(b.sT.h, mk[("hmm", it - 2)])
                        sKW[hp].dma(gp.dma_start(out=kwt[hp][:],
                                                  in_=b.kvk_in[h]))
                        sKW[hp].dma(gp.dma_start(out=vwt[hp][:],
                                                  in_=b.kvv_in[h]))
                        sKW[hp].dma(gp.dma_start(out=qwt[hp][:],
                                                  in_=b.qw_in[h]))
                    mk[("wld", it)] = (sKW[hp], sKW[hp].n)

                    base_waits = [(sX1, mk[("x1", half)]),
                                  (mk[("wld", it)][0], mk[("wld", it)][1])]
                    if it >= 2:
                        base_waits.append((b.sV, mk[("bfree", it - 2)]))

                    for tc in range(4):
                        w = list(base_waits) if tc == 0 else []
                        if tc >= 1:
                            w = [(b.sV, mk[("kev", it, tc - 1)])]

                        def et(tensor, hp=hp, tc=tc):
                            for dt in range(NT):
                                b.sT.inc(tensor.matmul(
                                    pk[0][:],
                                    kwt[hp][:, dt * 128:(dt + 1) * 128],
                                    x1F[:, dt * T + tc * 512:
                                        dt * T + (tc + 1) * 512],
                                    start=(dt == 0), stop=(dt == NT - 1)))

                        def ev(vector, hp=hp, tc=tc):
                            b.sV.inc(vector.tensor_copy(
                                kbuf[hp][:, tc * 512:(tc + 1) * 512], pk[0][:]))
                        TV(("kmm", it, tc), ("kev", it, tc), et, ev, w)

                    for tt in range(NT):
                        w = []
                        if tt >= 2:
                            w = [(b.sV, mk[("vev", it, tt - 2)])]

                        def et(tensor, hp=hp, tt=tt):
                            for dt in range(NT):
                                b.sT.inc(tensor.matmul(
                                    pv[tt % 2][:],
                                    x1F[:, dt * T + tt * 128:
                                        dt * T + (tt + 1) * 128],
                                    vwt[hp][:, dt * 128:(dt + 1) * 128],
                                    start=(dt == 0), stop=(dt == NT - 1)))

                        def ev(vector, hp=hp, tt=tt):
                            b.sV.inc(vector.tensor_copy(
                                vbuf[hp][:, tt * 128:(tt + 1) * 128],
                                pv[tt % 2][:]))
                        TV(("vmm", it, tt), ("vev", it, tt), et, ev, w)

                    if half == 1:
                        for tc in range(4):
                            w = []
                            if tc >= 1:
                                w = [(b.sV, mk[("qev", it, tc - 1)])]

                            def et(tensor, hp=hp, tc=tc):
                                for dt in range(NT):
                                    b.sT.inc(tensor.matmul(
                                        pk[1][:],
                                        qwt[hp][:, dt * 128:(dt + 1) * 128],
                                        x1F[:, dt * T + tc * 512:
                                            dt * T + (tc + 1) * 512],
                                        start=(dt == 0), stop=(dt == NT - 1)))

                            def ev(vector, hp=hp, tc=tc):
                                b.sV.inc(vector.tensor_single_scalar(
                                    qbuf[hp][:, tc * 512:(tc + 1) * 512],
                                    pk[1][:], b.cst[:, C_FSCale:C_FSCale + 1],
                                    op=ALU.mult))
                            TV(("qmm", it, tc), ("qev", it, tc), et, ev, w)
                    mk[("hmm", it)] = b.sT.n
                    mk[("hev", it)] = b.sV.n

                    @b.on("scalar")
                    def _(scalar, h=h, hp=hp, it=it):
                        scalar.wait_ge(b.sV.h, mk[("hev", it)])
                        if it >= 1:
                            scalar.wait_ge(b.sV.h, mk[("krow", it - 1)])
                        b.sS.inc(scalar.activation(out=ksq[:], in_=kbuf[hp][:],
                                                   func=AF.Square))
                    mk[("ksq", it)] = b.sS.n

                    for c in range(4):
                        @b.on("tensor")
                        def _(tensor, it=it, c=c):
                            tensor.wait_ge(b.sS.h, mk[("ksq", it)])
                            if c >= 1:
                                tensor.wait_ge(b.sV.h, mk[("kre", it, c - 1)])
                            b.sT.inc(tensor.matmul(
                                pr[:], b.onesc_f[:],
                                ksq[:, c * 512:(c + 1) * 512],
                                start=True, stop=True))
                        mk[("knrm", it, c)] = b.sT.n

                        @b.on("vector")
                        def _(vector, it=it, c=c):
                            vector.wait_ge(b.sT.h, mk[("knrm", it, c)])
                            b.sV.inc(vector.tensor_copy(
                                krow[0:1, c * 512:(c + 1) * 512], pr[:]))
                        mk[("kre", it, c)] = b.sV.n

                    @b.on("vector")
                    def _(vector, h=h, it=it, half=half):
                        vector.reduce_max(krow[0:1, 0:1], krow[:], axis=AX.X)
                        if half == 0:
                            ins = vector.tensor_copy(b.kmax[0:1, h:h + 1],
                                                     krow[0:1, 0:1])
                        else:
                            ins = vector.tensor_max(b.kmax[0:1, h:h + 1],
                                                    b.kmax[0:1, h:h + 1],
                                                    krow[0:1, 0:1])
                        b.sV.inc(ins)
                    mk[("krow", it)] = b.sV.n

                    @b.on("sync")
                    def _(sync, h=h, hp=hp, it=it, half=half):
                        sync.wait_ge(b.sV.h, mk[("hev", it)])
                        off = half * T
                        sST[hp].dma(sync.dma_start(
                            out=b.KF[h, :, off:off + T], in_=kbuf[hp][:]))
                        sST[hp].dma(sync.dma_start(
                            out=b.VTs[h, off:off + T, :].rearrange(
                                "(tt p) v -> p tt v", p=128),
                            in_=vbuf[hp].ap().rearrange(
                                "p (tt v) -> p tt v", v=128)))
                        if half == 1:
                            sST[hp].dma(sync.dma_start(
                                out=b.QS[h], in_=qbuf[hp][:]))
                    mk[("hst", it)] = (sST[hp], sST[hp].n)

                    @b.on("vector")
                    def _(vector, it=it):
                        vector.wait_ge(mk[("hst", it)][0].h, mk[("hst", it)][1])
                        b.sV.inc(vector.engine_nop())
                    mk[("bfree", it)] = b.sV.n
                mk[("allmm", half)] = b.sT.n

            @b.on("sync")
            def _(sync):
                sX1.wait(sync)
                sST[0].wait(sync)
                sST[1].wait(sync)
            @b.on("gpsimd")
            def _(gp):
                sKW[0].wait(gp)
                sKW[1].wait(gp)

    def emit_f2(self):
        b = self
        self.px_flash = ExitStack()
        OF = b.sb("f2_OF", [128, NH * T], BF16, ctx=self.px_flash)
        self.OF = OF
        with ExitStack() as px:
            kh_ = [b.sb(f"f2_kh{i}", [128, 2 * T], BF16, px) for i in range(2)]
            vh_ = [b.sb(f"f2_vh{i}", [128, 2 * T], BF16, px) for i in range(2)]
            qh_ = [b.sb(f"f2_qh{i}", [128, T], BF16, px) for i in range(2)]
            pbf = [b.sb(f"f2_pbf{i}", [128, 1024], BF16, px) for i in range(2)]
            qn2 = b.sb("f2_qn2", [128, 512], F32, px)
            srow = b.sb("f2_srow", [1, 512], F32, px)
            nmrow = b.sb("f2_nmrow", [1, 512], BF16, px)
            nmprow = b.sb("f2_nmprow", [1, 512], BF16, px)
            recb = b.sb("f2_recb", [128, 512], F32, px)
            kms = b.sb("f2_kms", [1, NH], F32, px)
            ps_ = [b.ps(f"f2_ps{i}", [128, 1024], F32, px) for i in range(2)]
            po = b.ps("f2_po", [128, 512], F32, px)
            pr = b.ps("f2_pr", [1, 512], F32, px)
            pb2 = b.ps("f2_pb2", [128, 512], F32, px)
            mk = {}
            sHL = [b.sem("f2_sHL0"), b.sem("f2_sHL1")]

            @b.on("scalar")
            def _(scalar):
                scalar.activation(out=kms[:], in_=b.kmax[:], func=AF.Ln)
                b.sS.inc(scalar.activation(out=kms[:], in_=kms[:], func=AF.Exp,
                                           scale=b.cst[0:1, C_HALF:C_HALF + 1]))
            mk["kms"] = b.sS.n

            prev = None
            for h in range(NH):
                hp = h % 2

                @b.on("sync")
                def _(sync, h=h, hp=hp):
                    if h >= 2:
                        sync.wait_ge(b.sT.h, mk[("hdone", h - 2)])
                    sHL[hp].dma(sync.dma_start(out=kh_[hp][:], in_=b.KF[h]))
                    sHL[hp].dma(sync.dma_start(
                        out=vh_[hp].ap().rearrange("p (tt v) -> p tt v", v=128),
                        in_=b.VTs[h].rearrange("(tt p) v -> p tt v", p=128)))
                    sHL[hp].dma(sync.dma_start(out=qh_[hp][:], in_=b.QS[h]))
                mk[("hld", h)] = (sHL[hp], sHL[hp].n)

                for qg in range(4):
                    nkt = 16 + (qg + 1) * 4
                    qsl = slice(qg * 512, (qg + 1) * 512)

                    @b.on("scalar")
                    def _(scalar, h=h, hp=hp, qg=qg, qsl=qsl, prev=prev):
                        s, v = mk[("hld", h)]
                        scalar.wait_ge(s.h, v)
                        if prev is not None:
                            scalar.wait_ge(b.sT.h, mk[("qnr",) + prev])
                        b.sS.inc(scalar.activation(out=qn2[:],
                                                   in_=qh_[hp][:, qsl],
                                                   func=AF.Square))
                    mk[("qn2", h, qg)] = b.sS.n

                    @b.on("tensor")
                    def _(tensor, h=h, qg=qg, prev=prev):
                        tensor.wait_ge(b.sS.h, mk[("qn2", h, qg)])
                        if prev is not None:
                            tensor.wait_ge(b.sV.h, mk[("sr1",) + prev])
                        b.sT.inc(tensor.matmul(pr[:], b.onesc_f[:], qn2[:],
                                               start=True, stop=True))
                    mk[("qnr", h, qg)] = b.sT.n

                    @b.on("vector")
                    def _(vector, h=h, qg=qg, prev=prev):
                        vector.wait_ge(b.sT.h, mk[("qnr", h, qg)])
                        if prev is not None:
                            vector.wait_ge(b.sT.h, mk[("recb",) + prev])
                        b.sV.inc(vector.tensor_copy(srow[:], pr[:]))
                    mk[("sr1", h, qg)] = b.sV.n

                    @b.on("scalar")
                    def _(scalar, h=h, qg=qg):
                        scalar.wait_ge(b.sV.h, mk[("sr1", h, qg)])
                        scalar.wait_ge(b.sS.h, mk["kms"])
                        scalar.activation(out=srow[:], in_=srow[:], func=AF.Ln)
                        b.sS.inc(scalar.activation(
                            out=srow[:], in_=srow[:], func=AF.Exp,
                            scale=b.cst[0:1, C_HALF:C_HALF + 1]))
                    mk[("sr2", h, qg)] = b.sS.n

                    @b.on("vector")
                    def _(vector, h=h, qg=qg, prev=prev):
                        vector.wait_ge(b.sS.h, mk[("sr2", h, qg)])
                        if prev is not None:
                            vector.wait_ge(b.sT.h, mk[("lastsmm",) + prev])
                        vector.tensor_single_scalar(
                            srow[:], srow[:], kms[0:1, h:h + 1], op=ALU.mult)
                        vector.tensor_single_scalar(
                            nmrow[:], srow[:], -1.0, op=ALU.mult)
                        b.sV.inc(vector.tensor_single_scalar(
                            nmprow[:], nmrow[:],
                            b.cst[0:1, C_PBIAS:C_PBIAS + 1], op=ALU.add))
                    mk[("nm", h, qg)] = b.sV.n

                    ngrp = nkt // 2
                    for gi in range(ngrp):
                        pp = gi % 2
                        kts = (2 * gi, 2 * gi + 1)
                        diags = tuple(kt - (16 + qg * 4) for kt in kts)

                        @b.on("tensor")
                        def _(tensor, h=h, hp=hp, qg=qg, gi=gi, pp=pp,
                              kts=kts, qsl=qsl, prev=prev):
                            if gi == 0:
                                tensor.wait_ge(b.sV.h, mk[("nm", h, qg)])
                                if prev is not None:
                                    tensor.wait_ge(b.sV.h, mk[("ofin",) + prev])
                            if gi >= 2:
                                tensor.wait_ge(b.sS.h, mk[("pexp", h, qg,
                                                           gi - 2)])
                            for j, kt in enumerate(kts):
                                dst = ps_[pp][:, j * 512:(j + 1) * 512]
                                b.sT.inc(tensor.matmul(
                                    dst, kh_[hp][:, kt * 128:(kt + 1) * 128],
                                    qh_[hp][:, qsl], start=True, stop=False))
                                b.sT.inc(tensor.matmul(
                                    dst, b.onesr_b[:],
                                    nmprow[:] if kt < 16 else nmrow[:],
                                    start=False, stop=True))
                        mk[("smm", h, qg, gi)] = b.sT.n

                        @b.on("scalar")
                        def _(scalar, h=h, qg=qg, gi=gi, pp=pp):
                            scalar.wait_ge(b.sT.h, mk[("smm", h, qg, gi)])
                            if gi >= 2:
                                scalar.wait_ge(b.sT.h, mk[("pv", h, qg,
                                                           gi - 2)])
                            b.sS.inc(scalar._eng.activation(
                                out=pbf[pp][:], in_=ps_[pp][:], func=AF.Exp))
                        mk[("pexp", h, qg, gi)] = b.sS.n

                        if max(diags) >= 0:
                            @b.on("vector")
                            def _(vector, h=h, qg=qg, gi=gi, diags=diags,
                                  pp=pp):
                                vector.wait_ge(b.sS.h, mk[("pexp", h, qg, gi)])
                                for j, dg in enumerate(diags):
                                    if dg >= 0:
                                        b.sV.inc(vector.tensor_mul(
                                            pbf[pp][:, j * 512:(j + 1) * 512],
                                            pbf[pp][:, j * 512:(j + 1) * 512],
                                            b.fmask[dg][:]))
                            mk[("pmask", h, qg, gi)] = b.sV.n

                        @b.on("tensor")
                        def _(tensor, h=h, hp=hp, qg=qg, gi=gi, pp=pp,
                              kts=kts, diags=diags,
                              last=(gi == nkt // 2 - 1), prev=prev):
                            if max(diags) >= 0:
                                tensor.wait_ge(b.sV.h, mk[("pmask", h, qg, gi)])
                            else:
                                tensor.wait_ge(b.sS.h, mk[("pexp", h, qg, gi)])
                            if gi == 0 and prev is not None:
                                tensor.wait_ge(b.sV.h, mk[("rec",) + prev])
                            for j, kt in enumerate(kts):
                                src = pbf[pp][:, j * 512:(j + 1) * 512]
                                b.sT.inc(tensor.matmul(
                                    pb2[0:1, 0:512], b.onesc_b[:], src,
                                    start=(kt == 0), stop=(last and j == 1)))
                                b.sT.inc(tensor.matmul(
                                    po[:], vh_[hp][:, kt * 128:(kt + 1) * 128],
                                    src, start=(kt == 0),
                                    stop=(last and j == 1)))
                        mk[("pv", h, qg, gi)] = b.sT.n
                    mk[("lastsmm", h, qg)] = mk[("smm", h, qg, ngrp - 1)]

                    @b.on("vector")
                    def _(vector, h=h, qg=qg):
                        vector.wait_ge(b.sT.h, mk[("pv", h, qg, nkt // 2 - 1)])
                        b.sV.inc(vector.reciprocal(srow[:], pb2[0:1, 0:512]))
                    mk[("rec", h, qg)] = b.sV.n

                    @b.on("tensor")
                    def _(tensor, h=h, qg=qg):
                        tensor.wait_ge(b.sV.h, mk[("rec", h, qg)])
                        b.sT.inc(tensor.matmul(pb2[:], b.onesr_f[:],
                                               srow[:], start=True, stop=True))
                    mk[("recb", h, qg)] = b.sT.n

                    @b.on("vector")
                    def _(vector, h=h, qg=qg):
                        vector.wait_ge(b.sT.h, mk[("recb", h, qg)])
                        vector.tensor_copy(recb[:], pb2[:])
                        b.sV.inc(vector.tensor_mul(
                            OF[:, h * T + qg * 512:h * T + (qg + 1) * 512],
                            po[:], recb[:]))
                    mk[("ofin", h, qg)] = b.sV.n
                    prev = (h, qg)
                mk[("hdone", h)] = b.sT.n

            @b.on("sync")
            def _(sync):
                sHL[0].wait(sync)
                sHL[1].wait(sync)

    def emit_f3(self):
        b = self
        OF = self.OF
        with ExitStack() as px:
            owb = b.sb("f3_owb", [128, NH * T], BF16, px)
            xnt = [b.sb(f"f3_xn{i}", [128, D], F32, px) for i in range(2)]
            r1t = [b.sb(f"f3_r1{i}", [128, D], F32, px) for i in range(2)]
            hid = [b.sb("f3_hid0", [128, D], F32, px)] * 2
            sq = [b.sb("f3_sq0", [128, D], F32, px)] * 2
            x2b = [b.sb(f"f3_x2b{i}", [128, D], BF16, px) for i in range(2)]
            stat = b.sb("f3_stat", [128, 64], F32, px)
            pc = [b.ps(f"f3_pc{i}", [128, 512], F32, px) for i in range(2)]
            mk = {}
            sOW = b.sem("f3_sOW")
            sXN = [b.sem("f3_sXN0"), b.sem("f3_sXN1")]
            sRS = [b.sem("f3_sRS0"), b.sem("f3_sRS1")]

            @b.on("gpsimd")
            def _(gp):
                sOW.dma(gp.dma_start(out=owb[:], in_=b.ow_in))

            for tc in range(NT):
                pp = tc % 2
                row = slice(tc * 128, (tc + 1) * 128)

                @b.on("sync")
                def _(sync, tc=tc, pp=pp, row=row):
                    if tc >= 2:
                        sync.wait_ge(b.sV.h, mk[("x2", tc - 2)])
                        s, v = mk[("rst", tc - 2)]
                        sync.wait_ge(s.h, v)
                    sXN[pp].dma(sync.dma_start(out=xnt[pp][:],
                                               in_=b.XN32[row, :]))
                    sXN[pp].dma(sync.dma_start(out=r1t[pp][:],
                                               in_=b.R1[row, :]))
                mk[("ld", tc)] = (sXN[pp], sXN[pp].n)

                for dc in range(4):
                    @b.on("tensor")
                    def _(tensor, tc=tc, dc=dc, pp=pp):
                        if tc == 0 and dc == 0:
                            sOW.wait(tensor)
                        if (tc, dc) == (0, 0):
                            pass
                        prev = (tc * 4 + dc - 2)
                        if prev >= 0:
                            tensor.wait_ge(b.sV.h, mk[("hev", prev)])
                        for fh in range(NH):
                            b.sT.inc(tensor.matmul(
                                pc[dc % 2][:],
                                OF[:, fh * T + tc * 128:fh * T + (tc + 1) * 128],
                                owb[:, fh * T + dc * 512:fh * T + (dc + 1) * 512],
                                start=(fh == 0), stop=(fh == NH - 1)))
                    mk[("omm", tc, dc)] = b.sT.n

                    @b.on("vector")
                    def _(vector, tc=tc, dc=dc, pp=pp):
                        vector.wait_ge(b.sT.h, mk[("omm", tc, dc)])
                        s, v = mk[("ld", tc)]
                        vector.wait_ge(s.h, v)
                        if tc >= 1 and dc == 0:
                            vector.wait_ge(b.sS.h, mk[("sq", tc - 1)])
                        b.sV.inc(vector.tensor_add(
                            hid[pp][:, dc * 512:(dc + 1) * 512],
                            pc[dc % 2][:],
                            xnt[pp][:, dc * 512:(dc + 1) * 512]))
                    mk[("hev", tc * 4 + dc)] = b.sV.n

                st = (tc % 2) * 4

                @b.on("vector")
                def _(vector, tc=tc, pp=pp, st=st):
                    vector.wait_ge(b.sV.h, mk[("hev", tc * 4 + 3)])
                    if tc >= 2:
                        vector.wait_ge(mk[("rst", tc - 2)][0].h,
                                       mk[("rst", tc - 2)][1])
                    b.sV.inc(vector.tensor_add(r1t[pp][:], r1t[pp][:],
                                               hid[pp][:]))
                mk[("radd", tc)] = b.sV.n

                @b.on("scalar")
                def _(scalar, tc=tc, pp=pp):
                    scalar.wait_ge(b.sV.h, mk[("radd", tc)])
                    if tc >= 1:
                        scalar.wait_ge(b.sV.h, mk[("x2", tc - 1)])
                    b.sS.inc(scalar.activation(out=sq[pp][:], in_=hid[pp][:],
                                               func=AF.Square))
                mk[("sq", tc)] = b.sS.n

                @b.on("vector")
                def _(vector, tc=tc, pp=pp, st=st):
                    vector.wait_ge(b.sS.h, mk[("sq", tc)])
                    b.sV.inc(vector.reduce_sum(stat[:, st:st + 1], sq[pp][:],
                                               axis=AX.X))
                mk[("s2", tc)] = b.sV.n

                @b.on("scalar")
                def _(scalar, tc=tc, st=st):
                    scalar.wait_ge(b.sV.h, mk[("s2", tc)])
                    scalar.activation(out=stat[:, st + 1:st + 2],
                                      in_=stat[:, st:st + 1], func=AF.Ln,
                                      scale=b.cst[:, C_INV_D:C_INV_D + 1],
                                      bias=b.cst[:, C_EPS:C_EPS + 1])
                    b.sS.inc(scalar.activation(
                        out=stat[:, st + 1:st + 2], in_=stat[:, st + 1:st + 2],
                        func=AF.Exp, scale=b.cst[:, C_NHALF:C_NHALF + 1]))
                mk[("rstd", tc)] = b.sS.n

                @b.on("vector")
                def _(vector, tc=tc, pp=pp, st=st):
                    vector.wait_ge(b.sS.h, mk[("rstd", tc)])
                    vector.tensor_single_scalar(sq[pp][:], hid[pp][:],
                                                stat[:, st + 1:st + 2],
                                                op=ALU.mult)
                    b.sV.inc(vector.tensor_mul(x2b[pp][:], sq[pp][:],
                                               b.ln2b[:]))
                mk[("x2", tc)] = b.sV.n

                @b.on("sync")
                def _(sync, tc=tc, pp=pp, row=row):
                    sync.wait_ge(b.sV.h, mk[("x2", tc)])
                    sRS[pp].dma(sync.dma_start(out=b.RSCR[row, :],
                                               in_=r1t[pp][:]))
                    sRS[pp].dma(sync.dma_start(out=b.X2BF[row, :],
                                               in_=x2b[pp][:]))
                mk[("rst", tc)] = (sRS[pp], sRS[pp].n)

            @b.on("sync")
            def _(sync):
                for s in (sXN[0], sXN[1], sRS[0], sRS[1]):
                    s.wait(sync)
            @b.on("gpsimd")
            def _(gp):
                sOW.wait(gp)
        self.px_flash.close()

    def emit_p4(self):
        b = self
        with ExitStack() as px:
            x2F = b.sb("p4_x2F", [128, NT * T], BF16, px)
            gwb = b.sb("p4_gwb", [128, NT * 16], BF16, px)
            tieb = b.sb("p4_tieb", [128, 16], F32, px)
            WF = b.sb("p4_WF", [128, NT * 16], F32, px)
            prob = b.sb("p4_prob", [128, 16], F32, px)
            tmp2 = b.sb("p4_tmp2", [128, 16], F32, px)
            eqt = b.sb("p4_eqt", [128, 16], F32, px)
            wsel = b.sb("p4_wsel", [128, 16], F32, px)
            gcol = b.sb("p4_gcol", [128, 4], F32, px)
            w1g = [b.sb(f"p4_w1g{i}", [128, NT * 128], BF16, px) for i in range(2)]
            w1u = [b.sb(f"p4_w1u{i}", [128, NT * 128], BF16, px) for i in range(2)]
            w2b = [b.sb("p4_w2b0", [128, 11 * T], BF16, px)] * 2
            act = b.sb("p4_act", [128, 11 * T], BF16, px)
            sgf = b.sb("p4_sgf", [128, 512], F32, px)
            t2f = b.sb("p4_t2f", [128, 512], F32, px)
            acin = [b.sb(f"p4_aci{i}", [128, 512], F32, px) for i in range(4)]
            acout = [b.sb(f"p4_aco{i}", [128, 512], F32, px) for i in range(4)]
            pg = [b.ps(f"p4_pg{i}", [128, 512], F32, px) for i in range(2)]
            pu = [b.ps(f"p4_pu{i}", [128, 512], F32, px) for i in range(2)]
            pdn = [b.ps(f"p4_pdn{i}", [128, 512], F32, px) for i in range(2)]
            pxg = ExitStack()
            pgate = b.ps("p4_pgate", [128, 16], F32, pxg)
            mk = {}
            sX2 = b.sem("p4_sX2")
            sGW = b.sem("p4_sGW")
            sW1 = [b.sem("p4_sW10"), b.sem("p4_sW11")]
            sW2 = [b.sem("p4_sW20"), b.sem("p4_sW21")]
            sAL = [b.sem(f"p4_sAL{i}") for i in range(4)]
            sAS = [b.sem(f"p4_sAS{i}") for i in range(4)]
            sOUT = b.sem("p4_sOUT")
            last_w1_user = {}

            @b.on("sync")
            def _(sync):
                for dt in range(NT):
                    sX2.dma(sync.dma_start(
                        out=x2F[:, dt * T:(dt + 1) * T],
                        in_=b.X2BF[:, dt * 128:(dt + 1) * 128],
                        transpose=True))
                sX2.dma(sync.dma_start(out=tieb[:], in_=b.tieb_in))
            mk["x2"] = sX2.n

            @b.on("gpsimd")
            def _(gp):
                sGW.dma(gp.dma_start(out=gwb[:], in_=b.gw_in))
            mk["gw"] = sGW.n

            for tc in range(NT):
                @b.on("tensor")
                def _(tensor, tc=tc):
                    if tc == 0:
                        tensor.wait_ge(sX2.h, mk["x2"])
                        tensor.wait_ge(sGW.h, mk["gw"])
                    if tc >= 1:
                        tensor.wait_ge(b.sS.h, mk[("gprob", tc - 1)])
                    for dt in range(NT):
                        b.sT.inc(tensor.matmul(
                            pgate[:], x2F[:, dt * T + tc * 128:
                                          dt * T + (tc + 1) * 128],
                            gwb[:, dt * 16:(dt + 1) * 16],
                            start=(dt == 0), stop=(dt == NT - 1)))
                mk[("gmm", tc)] = b.sT.n

                @b.on("vector")
                def _(vector, tc=tc):
                    vector.wait_ge(b.sT.h, mk[("gmm", tc)])
                    if tc >= 1:
                        vector.wait_ge(b.sV.h, mk[("top4", tc - 1)])
                    b.sV.inc(vector.reduce_max(gcol[:, 0:1], pgate[:],
                                               axis=AX.X, negate=True))
                mk[("gmax", tc)] = b.sV.n

                @b.on("scalar")
                def _(scalar, tc=tc):
                    scalar.wait_ge(b.sV.h, mk[("gmax", tc)])
                    b.sS.inc(scalar.activation(out=prob[:], in_=pgate[:],
                                               func=AF.Exp,
                                               bias=gcol[:, 0:1]))
                mk[("gprob", tc)] = b.sS.n

                @b.on("vector")
                def _(vector, tc=tc):
                    vector.wait_ge(b.sS.h, mk[("gprob", tc)])
                    vector.tensor_mul(tmp2[:], prob[:], tieb[:])
                    vector.memset(wsel[:], 0.0)
                    for r in range(4):
                        vector.reduce_max(gcol[:, 1:2], tmp2[:], axis=AX.X)
                        vector.tensor_single_scalar(eqt[:], tmp2[:],
                                                    gcol[:, 1:2],
                                                    op=ALU.is_equal)
                        vector.scalar_tensor_tensor(
                            out=tmp2[:], in0=eqt[:], scalar=-1e30, in1=tmp2[:],
                            op0=ALU.mult, op1=ALU.add)
                        vector.tensor_mul(eqt[:], eqt[:], prob[:])
                        vector.tensor_add(wsel[:], wsel[:], eqt[:])
                    vector.reduce_sum(gcol[:, 2:3], wsel[:], axis=AX.X)
                    vector.reciprocal(gcol[:, 3:4], gcol[:, 2:3])
                    b.sV.inc(vector.tensor_single_scalar(
                        WF[:, tc * 16:(tc + 1) * 16], wsel[:], gcol[:, 3:4],
                        op=ALU.mult))
                mk[("top4", tc)] = b.sV.n
            mk["gate_done"] = b.sV.n
            if "DBG_WF" in b.dbg:
                @b.on("sync")
                def _(sync):
                    sync.wait_ge(b.sV.h, mk["gate_done"])
                    sOUT.dma(sync.dma_start(out=b.DBG_WF, in_=WF[:]))
            pxg.close()
            pdn += [b.ps(f"p4_pdn{i}", [128, 512], F32, px) for i in (2, 3)]

            for e in range(18):
                ep = e % 2
                shared = e < 2
                k = e - 2
                if shared:
                    gblk = lambda it, e=e: b.shg_in[e * 11 + it]
                    ublk = lambda it, e=e: b.shg_in[22 + e * 11 + it]
                    dn_ap = b.shd_in[e]
                else:
                    gblk = lambda it, k=k: b.w1_in[k, it]
                    ublk = lambda it, k=k: b.w1_in[k, 11 + it]
                    dn_ap = b.w2_in[k]
                acc_in = [b.RSCR, b.ACC, b.ACC2][0 if e == 0 else
                                                (1 if e % 2 == 1 else 2)]
                acc_out = [b.ACC, b.ACC2][0 if e % 2 == 0 else 1]
                last = e == 17

                for it in range(11):
                    wp = it % 2

                    @b.on("gpsimd")
                    def _(gp, e=e, it=it, wp=wp, gblk=gblk, ublk=ublk):
                        if wp in last_w1_user:
                            gp.wait_ge(b.sT.h, mk[last_w1_user[wp]])
                        sW1[wp].dma(gp.dma_start(out=w1g[wp][:], in_=gblk(it)))
                        sW1[wp].dma(gp.dma_start(out=w1u[wp][:], in_=ublk(it)))
                    mk[("w1ld", e, it)] = (sW1[wp], sW1[wp].n)

                    for tc4 in range(4):
                        ai = it * 4 + tc4
                        ap_ = ai % 2

                        @b.on("tensor")
                        def _(tensor, e=e, it=it, wp=wp, tc4=tc4, ai=ai,
                              ap_=ap_):
                            if tc4 == 0:
                                s, v = mk[("w1ld", e, it)]
                                tensor.wait_ge(s.h, v)
                                if e == 0 and it == 0:
                                    tensor.wait_ge(b.sV.h, mk["gate_done"])
                            if ai >= 2:
                                tensor.wait_ge(b.sV.h, mk[("actev", e,
                                                           (ai - 2) // 4,
                                                           (ai - 2) % 4)])
                            elif e >= 1:
                                tensor.wait_ge(b.sV.h, mk[("actev", e - 1, 10,
                                                           2 + ai)])
                            for dt in range(NT):
                                b.sT.inc(tensor.matmul(
                                    pg[ap_][:],
                                    w1g[wp][:, dt * 128:(dt + 1) * 128],
                                    x2F[:, dt * T + tc4 * 512:
                                        dt * T + (tc4 + 1) * 512],
                                    start=(dt == 0), stop=(dt == NT - 1)))
                            for dt in range(NT):
                                b.sT.inc(tensor.matmul(
                                    pu[ap_][:],
                                    w1u[wp][:, dt * 128:(dt + 1) * 128],
                                    x2F[:, dt * T + tc4 * 512:
                                        dt * T + (tc4 + 1) * 512],
                                    start=(dt == 0), stop=(dt == NT - 1)))
                        mk[("actmm", e, it, tc4)] = b.sT.n

                        @b.on("scalar")
                        def _(scalar, e=e, it=it, tc4=tc4, ai=ai, ap_=ap_):
                            scalar.wait_ge(b.sT.h, mk[("actmm", e, it, tc4)])
                            if ai >= 1:
                                scalar.wait_ge(b.sV.h, mk[("actev", e,
                                                           (ai - 1) // 4,
                                                           (ai - 1) % 4)])
                            b.sS.inc(scalar._eng.activation(out=sgf[:],
                                                            in_=pg[ap_][:],
                                                            func=AF.Silu))
                        mk[("actsg", e, it, tc4)] = b.sS.n

                        @b.on("vector")
                        def _(vector, e=e, it=it, tc4=tc4, ap_=ap_):
                            vector.wait_ge(b.sS.h, mk[("actsg", e, it, tc4)])
                            vector.tensor_mul(t2f[:], sgf[:], pu[ap_][:])
                            b.sV.inc(vector.tensor_copy(
                                act[:, it * T + tc4 * 512:
                                    it * T + (tc4 + 1) * 512], t2f[:]))
                        mk[("actev", e, it, tc4)] = b.sV.n
                    mk[("w1used", e, it)] = mk[("actmm", e, it, 3)]
                    last_w1_user[wp] = ("w1used", e, it)
                if e == 2 and "DBG_ACT" in b.dbg:
                    @b.on("sync")
                    def _(sync):
                        sync.wait_ge(b.sV.h, mk[("actev", 2, 10, 3)])
                        sOUT.dma(sync.dma_start(out=b.DBG_ACT, in_=act[:]))

                @b.on("gpsimd")
                def _(gp, e=e, ep=ep, dn_ap=dn_ap):
                    if e >= 1:
                        gp.wait_ge(b.sT.h, mk[("dnmm_last", e - 1)])
                    sW2[ep].dma(gp.dma_start(out=w2b[ep][:], in_=dn_ap))
                mk[("w2ld", e)] = (sW2[ep], sW2[ep].n)

                for tc in range(NT):
                    for dc in range(4):
                        idx = tc * 4 + dc
                        dp = idx % 4
                        row = slice(tc * 128, (tc + 1) * 128)
                        col = slice(dc * 512, (dc + 1) * 512)

                        @b.on("sync")
                        def _(sync, e=e, idx=idx, dp=dp, row=row, col=col,
                              acc_in=acc_in):
                            if e >= 1:
                                s, v = mk[("ast", e - 1, idx)]
                                sync.wait_ge(s.h, v)
                            if idx >= 4:
                                sync.wait_ge(b.sV.h, mk[("dnev", e, idx - 4)])
                            sAL[dp].dma(sync.dma_start(out=acin[dp][:],
                                                       in_=acc_in[row, col]))
                        mk[("ald", e, idx)] = (sAL[dp], sAL[dp].n)

                        @b.on("tensor")
                        def _(tensor, e=e, ep=ep, idx=idx, dp=dp, tc=tc, dc=dc):
                            if idx == 0:
                                s, v = mk[("w2ld", e)]
                                tensor.wait_ge(s.h, v)
                            if dc == 0 and tc % 4 == 0:
                                tensor.wait_ge(b.sV.h,
                                               mk[("actev", e, 10, tc // 4)])
                            if idx >= 4:
                                tensor.wait_ge(b.sV.h, mk[("dnev", e, idx - 4)])
                            for it in range(11):
                                b.sT.inc(tensor.matmul(
                                    pdn[dp][:],
                                    act[:, it * T + tc * 128:
                                        it * T + (tc + 1) * 128],
                                    w2b[ep][:, it * T + dc * 512:
                                            it * T + (dc + 1) * 512],
                                    start=(it == 0), stop=(it == 10)))
                        mk[("dnmm", e, idx)] = b.sT.n

                        @b.on("vector")
                        def _(vector, e=e, idx=idx, dp=dp, tc=tc, k=k,
                              shared=shared):
                            vector.wait_ge(b.sT.h, mk[("dnmm", e, idx)])
                            s, v = mk[("ald", e, idx)]
                            vector.wait_ge(s.h, v)
                            if idx >= 4:
                                s2, v2 = mk[("ast", e, idx - 4)]
                                vector.wait_ge(s2.h, v2)
                            if shared:
                                ins = vector.tensor_add(acout[dp][:],
                                                        pdn[dp][:], acin[dp][:])
                            else:
                                ins = vector.scalar_tensor_tensor(
                                    out=acout[dp][:], in0=pdn[dp][:],
                                    scalar=WF[:, tc * 16 + k:tc * 16 + k + 1],
                                    in1=acin[dp][:],
                                    op0=ALU.mult, op1=ALU.add)
                            b.sV.inc(ins)
                        mk[("dnev", e, idx)] = b.sV.n

                        @b.on("sync")
                        def _(sync, e=e, idx=idx, dp=dp, row=row, col=col,
                              acc_out=acc_out, last=last):
                            sync.wait_ge(b.sV.h, mk[("dnev", e, idx)])
                            dst = b.out_t if last else acc_out
                            sAS[dp].dma(sync.dma_start(out=dst[row, col],
                                                       in_=acout[dp][:]))
                        mk[("ast", e, idx)] = (sAS[dp], sAS[dp].n)
                mk[("dnmm_last", e)] = b.sT.n

            @b.on("sync")
            def _(sync):
                for s in (sX2, *sAL, *sAS, sOUT):
                    s.wait(sync)
            @b.on("gpsimd")
            def _(gp):
                for s in (sGW, sW1[0], sW1[1], sW2[0], sW2[1]):
                    s.wait(gp)


def host_prep(inputs):
    f32 = np.float32
    x = np.ascontiguousarray(inputs["L0_attn_x"])
    Wa = inputs["L0_attn_Wa"]
    Wb = inputs["L0_attn_Wb"]
    wab = np.ascontiguousarray(np.concatenate([Wa, Wb], axis=1))
    dtb16 = np.ascontiguousarray(np.tile(
        np.asarray(inputs["L0_attn_dt_bias"], f32), NT)[None, :])
    alog16 = np.ascontiguousarray(np.tile(
        np.asarray(inputs["L0_attn_A_log"], f32), NT)[None, :])
    normw = np.ascontiguousarray(
        np.asarray(inputs["L0_attn_norm_w"], f32)[:, None])
    def colblocks(WT):
        Din, F = WT.shape
        return np.ascontiguousarray(
            WT.reshape(Din // 128, 128, F // 128, 128)
            .transpose(2, 1, 0, 3).reshape(F // 128, 128, Din // 128 * 128))

    def rowflat(WT):
        Din, F = WT.shape
        return np.ascontiguousarray(
            WT.reshape(Din // 128, 128, F).transpose(1, 0, 2)
            .reshape(128, Din // 128 * F))

    q_w = np.asarray(inputs["L0_ffn_q_w"], f32)
    qwT = np.ascontiguousarray(
        q_w.reshape(NH, QHD, D)[:, :NOPE, :].reshape(NH * NOPE, D).T)
    kv_w = np.asarray(inputs["L0_ffn_kv_w"], f32)
    kvwT = np.ascontiguousarray(kv_w[:2 * NH * NOPE, :].T)
    qw_r = colblocks(qwT)
    kvk_r = colblocks(kvwT[:, :NH * NOPE])
    kvv_r = colblocks(kvwT[:, NH * NOPE:])
    ow_r = rowflat(np.ascontiguousarray(np.asarray(inputs["L0_ffn_o_w"], f32).T))
    gwT_ = np.ascontiguousarray(np.asarray(inputs["L0_ffn_gate_w"], f32).T)
    gw_r = np.ascontiguousarray(
        gwT_.reshape(16, 128, 16).transpose(1, 0, 2).reshape(128, 256))
    w1T = np.asarray(inputs["L0_ffn_w1"], f32).transpose(0, 2, 1)
    w1_r = np.ascontiguousarray(np.stack([colblocks(w1T[e])
                                          for e in range(16)]))
    w2T = np.asarray(inputs["L0_ffn_w2"], f32).transpose(0, 2, 1)
    w2_r = np.ascontiguousarray(np.stack([rowflat(w2T[e]) for e in range(16)]))
    shg_r = colblocks(np.ascontiguousarray(
        np.asarray(inputs["L0_ffn_sh_gu_w"], f32).T))
    shdT = np.ascontiguousarray(np.asarray(inputs["L0_ffn_sh_down_w"], f32).T)
    shd_r = np.ascontiguousarray(np.stack(
        [rowflat(shdT[e * 1408:(e + 1) * 1408, :]) for e in range(2)]))

    sel = np.kron(np.eye(32, dtype=f32), np.ones((1, 128), f32))
    import ml_dtypes as _mld
    id4 = np.kron(np.ones((1, 4), f32), np.eye(128, dtype=f32))
    tri = np.tril(np.ones((128, 128), f32))
    mskui4 = np.kron(np.ones((1, 4), f32), tri.T)
    mskus4 = np.kron(np.ones((1, 4), f32), np.triu(np.ones((128, 128), f32), 1))
    mskls4 = np.kron(np.ones((1, 4), f32), np.tril(np.ones((128, 128), f32), -1))
    tieb = np.tile(1.0 + (15.0 - np.arange(16, dtype=f32)) * 1e-6, (128, 1))
    shared = dict(
        sel=np.ascontiguousarray(sel),
        id4=np.ascontiguousarray(id4),
        mskui4=np.ascontiguousarray(mskui4.astype(_mld.bfloat16)),
        mskus4=np.ascontiguousarray(mskus4.astype(_mld.bfloat16)),
        mskls4=np.ascontiguousarray(mskls4.astype(_mld.bfloat16)),
        tieb=np.ascontiguousarray(tieb.astype(f32)),
        dtb16=dtb16, alog16=alog16, normw=normw,
        ln1=np.ascontiguousarray(np.asarray(inputs["L0_ffn_ln1_w"], f32)[None, :]),
        ln2=np.ascontiguousarray(np.asarray(inputs["L0_ffn_ln2_w"], f32)[None, :]),
        convw=np.ascontiguousarray(np.asarray(inputs["L0_attn_conv_w"], f32)),
        wqkv=colblocks(np.asarray(inputs["L0_attn_Wqkv"], f32)),
        wz=colblocks(np.asarray(inputs["L0_attn_Wz"], f32)),
        wab=np.ascontiguousarray(
            wab.reshape(16, 128, 64).transpose(1, 0, 2).reshape(128, 1024)),
        wout=np.ascontiguousarray(
            rowflat(np.asarray(inputs["L0_attn_Wout"], f32))
            .reshape(128, HV, 4, 512).transpose(2, 0, 1, 3)
            .reshape(4, 128, HV * 512)),
        qw=qw_r, kvk=kvk_r, kvv=kvv_r, ow=ow_r, gw=gw_r,
        w1r=w1_r, w2r=w2_r, shgr=shg_r, shdr=shd_r,
    )
    cus = np.asarray(inputs["L0_attn_cu_seqlens"]).ravel()
    assert list(cus) == [0, 2048, 4096, 6144, 8192], (
        "kernel layout requires 2048-aligned segments, got %r" % (cus,))
    cuseq = np.zeros((1, 8), np.int32)
    cuseq[0, :cus.size] = cus
    pos = np.zeros((1, 64), np.int32)
    posv = np.asarray(inputs["L0_ffn_positions"], np.int32).ravel()[:64]
    pos[0, :posv.size] = posv
    in_maps = []
    for c in range(8):
        bt, hf = c // 2, c % 2
        cst = np.zeros((128, 16), f32)
        cst[:, C_LNEPS] = LN_EPS
        cst[:, C_EPS] = EPS
        cst[:, C_NHALF] = -0.5
        cst[:, C_QSCL] = DK ** -0.5
        cst[:, C_INV_D] = 1.0 / D
        cst[:, C_INV_DV] = 1.0 / DV
        cst[:, C_NEG1E9] = -1e9
        cst[:, C_PBIAS] = 0.0 if hf == 1 else -1e9
        cst[:, C_FSCale] = QHD ** -0.5
        cst[:, C_NEG1] = -1.0
        cst[:, C_HALF] = 0.5
        m = dict(shared)
        m["cuseq"] = cuseq
        m["pos"] = pos
        m["cst"] = cst
        m["x_own"] = np.ascontiguousarray(x[bt, hf * T:(hf + 1) * T, :])
        m["x_oth"] = np.ascontiguousarray(x[bt, (1 - hf) * T:(2 - hf) * T, :])
        in_maps.append(m)
    return in_maps


_BUILT = None


def kernel(**inputs):
    global _BUILT
    if _BUILT is None:
        _BUILT = B().build()
    nc = _BUILT
    in_maps = host_prep(inputs)
    res = run_bass_kernel_spmd(nc, in_maps, core_ids=list(range(8)))
    out = np.empty((4, 2 * T, D), np.float32)
    for c in range(8):
        bt, hf = c // 2, c % 2
        out[bt, hf * T:(hf + 1) * T, :] = res.results[c]["out"]
    return out
